# version 15
# baseline (speedup 1.0000x reference)
import contextlib
import numpy as np
import concourse.bass as bass
import concourse.mybir as mybir
from concourse.bass_utils import run_bass_kernel_spmd

F32 = mybir.dt.float32
BF16 = mybir.dt.bfloat16
AF = mybir.ActivationFunctionType
ALU = mybir.AluOpType

B, L, D, I, NH = 16, 2048, 2048, 8192, 16
NCORES = 8
TOK = (B // NCORES) * L
TILE = 512
CH = TILE // 128
DC = D // 128
NT = TOK // TILE
TPS = L // TILE
LN_EPS = 1e-5
LN2 = float(np.log(2.0))
TINY = float(np.finfo(np.float32).tiny)

ENG_SEM = {"tensor": "te", "vector": "ve", "scalar": "se"}
DMA_SEMS = (["qc", "qwa0", "qwa1", "qwb0", "qwb1", "qwd0", "qwd1", "qrc"]
            + ["qsi%d" % c for c in range(4)] + ["qsp%d" % c for c in range(4)]
            + ["qx%d" % c for c in range(4)] + ["qo%d" % c for c in range(4)])
ALL_ENGS = ("tensor", "vector", "scalar", "sync", "gpsimd")


def _bc(ap, pos, n):
    dims = [list(d) for d in ap.ap]
    dims.insert(pos, [0, n])
    return bass.AP(ap.tensor, ap.offset, dims)


def _stride4(t, col0, i, cnt):
    base = t[:, :]
    return bass.AP(t, base.offset + col0 + i, [list(base.ap[0]), [4, cnt]])


def _flat(dram, k, width):
    return bass.AP(dram, k * 128 * width, [[width, 128], [1, width]])


class Prog:

    def __init__(self):
        self.stream = {e: [] for e in ALL_ENGS}
        self.cnt = {s: 0 for s in list(ENG_SEM.values()) + DMA_SEMS}
        self.last_w = {}
        self.readers = {}
        self.mark = {}

    def op(self, eng, fns, reads=(), writes=(), sem=None):
        if not isinstance(fns, (list, tuple)):
            fns = [fns]
        if sem is None:
            sem = ENG_SEM[eng]
        inc = 16 if sem in DMA_SEMS else 1
        deps = []
        for r in reads:
            lw = self.last_w.get(r)
            if lw:
                deps.append(lw)
        for w in writes:
            lw = self.last_w.get(w)
            if lw:
                deps.append(lw)
            deps.extend(self.readers.get(w, ()))
        for dsem, dval in deps:
            if dsem == sem and eng == "tensor":
                continue
            if dsem in DMA_SEMS:
                dval = self.cnt[dsem]
            key = (eng, dsem)
            if self.mark.get(key, 0) < dval:
                self.mark[key] = dval
                self.stream[eng].append(("wait", dsem, dval))
        self.stream[eng].append(("ops", fns, sem, inc))
        self.cnt[sem] += inc
        pt = (sem, self.cnt[sem])
        for w in writes:
            self.last_w[w] = pt
            self.readers[w] = []
        for r in reads:
            self.readers.setdefault(r, []).append(pt)

    def materialize(self, nc, sems):
        with nc.Block() as blk:
            for eng in ALL_ENGS:
                entries = self.stream[eng]

                def make(entries):
                    def body(E):
                        for ent in entries:
                            if ent[0] == "wait":
                                E.wait_ge(sems[ent[1]], ent[2])
                            else:
                                inst = None
                                for f in ent[1]:
                                    inst = f(E)
                                inst.then_inc(sems[ent[2]], ent[3])
                    return body

                getattr(blk, eng)(make(entries))


class NS:
    pass


def build_nc(nt=NT):
    nc = bass.Bass("TRN2", target_bir_lowering=False)
    t = NS()
    t.x_d = nc.dram_tensor("x", [TOK, D], F32, kind="ExternalInput")
    t.out_d = nc.dram_tensor("out", [TOK, D], F32, kind="ExternalOutput")
    t.wqt_d = nc.dram_tensor("wqt", [D, D], F32, kind="ExternalInput")
    t.wvt_d = nc.dram_tensor("wvt", [D, D], F32, kind="ExternalInput")
    t.wgt_d = nc.dram_tensor("wgt", [D, D], F32, kind="ExternalInput")
    t.wot_d = nc.dram_tensor("wot", [D, D], F32, kind="ExternalInput")
    t.wga_d = nc.dram_tensor("wgate", [D, I], F32, kind="ExternalInput")
    t.wup_d = nc.dram_tensor("wup", [D, I], F32, kind="ExternalInput")
    t.wdn_d = nc.dram_tensor("wdown", [I, D], F32, kind="ExternalInput")
    for nm, shape in (("c_ident", [128, 128]), ("c_dts", [128, 128]),
                      ("c_iotap1", [128, 128]), ("c_maskut", [128, 128]),
                      ("c_oh127", [128, 1]), ("c_ones", [1, 128]),
                      ("c_decay_beta", [1, NH]), ("c_lngamma", [1, D]),
                      ("c_lnbeta", [1, D])):
        setattr(t, nm, nc.dram_tensor(nm, shape, F32, kind="ExternalInput"))
    for nm, srcd in (("wqt", t.wqt_d), ("wvt", t.wvt_d), ("wgt", t.wgt_d),
                     ("wot", t.wot_d), ("wga", t.wga_d), ("wup", t.wup_d),
                     ("wdn", t.wdn_d)):
        r, c = srcd.shape
        setattr(t, nm + "_s", nc.dram_tensor("s_%s" % nm, [r, 2 * c], BF16))
    t.retc_d = nc.dram_tensor("s_retc", [128, D], F32)

    cm = contextlib.ExitStack()

    def sb(name, shape, dt=F32):
        return cm.enter_context(nc.sbuf_tensor(name, shape, dt))

    with cm:
        t.x_sb = sb("x_sb", [128, CH * D])
        t.va = sb("va", [128, CH * D])
        t.lnH = sb("lnH", [128, CH * D], BF16)
        t.lnL = sb("lnL", [128, CH * D], BF16)
        t.hg = sb("hg", [128, 2 * D])
        t.hsH = sb("hsH", [128, 2 * D], BF16)
        t.hsL = sb("hsL", [128, 2 * D], BF16)
        t.wa = sb("wa", [128, 2 * 2048], BF16)
        t.wb = sb("wb", [128, 2 * 2048], BF16)
        t.wd = sb("wd", [128, 2 * 4096], BF16)
        t.scr = sb("scr", [128, D])
        t.scg = sb("scg", [128, 512])
        t.wt = sb("wt", [128, 256])
        t.a_t = sb("a_t", [128, NH * 128])
        t.sp = sb("sp", [128, NH * 128])
        t.gbc = sb("gbc", [128, D])
        t.bbc = sb("bbc", [128, D])
        t.ident = sb("ident", [128, 128])
        t.oh127 = sb("oh127", [128, 1])
        t.ones1 = sb("ones1", [1, 128])
        t.lamv = sb("lamv", [1, NH])
        t.lnl = sb("lnl", [1, NH])
        t.lnl_bc = sb("lnl_bc", [128, NH])
        t.st6 = sb("st6", [128, 24])
        t.ag2 = sb("ag2", [128, 2])
        t.rstd = sb("rstd", [128, 1])
        t.epsv = sb("epsv", [128, 1])
        t.pb = [cm.enter_context(nc.psum_tensor("pb%d" % i, [128, 512], F32))
                for i in range(8)]

        sems = {s: cm.enter_context(nc.semaphore(s))
                for s in list(ENG_SEM.values()) + DMA_SEMS}

        p = Prog()
        t.par = 0
        t.weave = []
        t.dpar = 0
        emit_setup(p, t)
        for c in range(CH):
            p.op("sync", lambda E, c=c: E.dma_start(
                t.x_sb[:, c * D:(c + 1) * D], t.x_d[c * 128:(c + 1) * 128, :]),
                 writes=["x:%d" % c], sem="qx%d" % c)
        emit_weight_split(p, t)
        for k in range(nt):
            emit_tile(p, t, k)
        for c in range(4):
            p.stream["gpsimd"].append(("wait", "qo%d" % c, p.cnt["qo%d" % c]))
        p.materialize(nc, sems)
    return nc



def emit_setup(p, t):
    dts = t.hg[:, 0:128]
    iop1 = t.hg[:, 128:256]
    maskut = t.hg[:, 256:384]
    loads = [(t.ident[:, :], t.c_ident, "ident"), (dts, t.c_dts, "hg:0"),
             (iop1, t.c_iotap1, "hg:1"), (maskut, t.c_maskut, "hg:2"),
             (t.oh127[:, :], t.c_oh127, "oh127"), (t.ones1[:, :], t.c_ones, "ones1"),
             (t.lamv[:, :], t.c_decay_beta, "lamv")]
    for s, d, key in loads:
        p.op("sync", (lambda E, s=s, d=d: E.dma_start(s, d[:, :])), writes=[key],
             sem="qc")
    p.op("sync", lambda E: E.dma_start(t.scr[0:1, :], t.c_lngamma[:, :]), writes=["scr"],
         sem="qc")

    p.op("vector", lambda E: E.memset(t.epsv[:, :], LN_EPS), writes=["epsv"])
    p.op("scalar", lambda E: E.activation(t.lamv[:, :], t.lamv[:, :], AF.Exp, scale=-LN2),
         reads=["lamv"], writes=["lamv"])
    p.op("vector", lambda E: E.tensor_scalar(t.lamv[:, :], t.lamv[:, :], 1.0, -1.0,
                                             op0=ALU.subtract, op1=ALU.mult),
         reads=["lamv"], writes=["lamv"])
    p.op("vector", lambda E: E.tensor_scalar(t.lamv[:, :], t.lamv[:, :], TINY, 1.0 - 1e-9,
                                             op0=ALU.max, op1=ALU.min),
         reads=["lamv"], writes=["lamv"])
    p.op("scalar", lambda E: E.activation(t.lnl[:, :], t.lamv[:, :], AF.Ln),
         reads=["lamv"], writes=["lnl"])

    p.op("tensor", lambda E: E.matmul(t.pb[0][:, 0:NH], t.ones1[0:1, :], t.lnl[0:1, :],
                                      start=True, stop=True),
         reads=["lnl", "ones1"], writes=["pb0"])
    p.op("vector", lambda E: E.tensor_copy(t.lnl_bc[:, :], t.pb[0][:, 0:NH]),
         reads=["pb0"], writes=["lnl_bc"])
    for j in range(4):
        s0, s1 = j * 512, (j + 1) * 512
        p.op("tensor", lambda E, s0=s0, s1=s1: E.matmul(
            t.pb[1][:, :], t.ones1[0:1, :], t.scr[0:1, s0:s1], start=True, stop=True),
             reads=["scr", "ones1"], writes=["pb1"])
        p.op("vector", lambda E, s0=s0, s1=s1: E.tensor_copy(t.gbc[:, s0:s1], t.pb[1][:, :]),
             reads=["pb1"], writes=["gbc"])
    p.op("sync", lambda E: E.dma_start(t.scr[0:1, :], t.c_lnbeta[:, :]), writes=["scr"],
         sem="qc")
    for j in range(4):
        s0, s1 = j * 512, (j + 1) * 512
        p.op("tensor", lambda E, s0=s0, s1=s1: E.matmul(
            t.pb[2][:, :], t.ones1[0:1, :], t.scr[0:1, s0:s1], start=True, stop=True),
             reads=["scr", "ones1"], writes=["pb2"])
        p.op("vector", lambda E, s0=s0, s1=s1: E.tensor_copy(t.bbc[:, s0:s1], t.pb[2][:, :]),
             reads=["pb2"], writes=["bbc"])

    a3 = t.a_t[:, :].rearrange("p (h u) -> p h u", h=NH)
    s3 = t.sp[:, :].rearrange("p (h u) -> p h u", h=NH)
    lnl3 = _bc(t.lnl_bc[:, :], 2, 128)
    p.op("vector", lambda E, dts=dts: E.tensor_tensor(a3, _bc(dts, 1, NH), lnl3, op=ALU.mult),
         reads=["hg:0", "lnl_bc"], writes=["a_t"])
    p.op("scalar", lambda E: E.activation(t.a_t[:, :], t.a_t[:, :], AF.Exp),
         reads=["a_t"], writes=["a_t"])
    p.op("vector", lambda E, maskut=maskut: E.tensor_tensor(a3, a3, _bc(maskut, 1, NH), op=ALU.mult),
         reads=["a_t", "hg:2"], writes=["a_t"])
    p.op("vector", lambda E, iop1=iop1: E.tensor_tensor(s3, _bc(iop1, 1, NH), lnl3, op=ALU.mult),
         reads=["hg:1", "lnl_bc"], writes=["sp"])
    p.op("scalar", lambda E: E.activation(t.sp[:, :], t.sp[:, :], AF.Exp),
         reads=["sp"], writes=["sp"])
    oh3 = bass.AP(t.oh127, 0, [list(t.oh127[:, :].ap[0]), [0, NH], [0, 128]])
    p.op("vector", lambda E: E.tensor_tensor(s3, s3, oh3, op=ALU.mult),
         reads=["sp", "oh127"], writes=["sp"])


def emit_weight_split(p, t):
    jobs = []
    for src_d, dst in ((t.wqt_d, t.wqt_s), (t.wvt_d, t.wvt_s),
                       (t.wgt_d, t.wgt_s), (t.wot_d, t.wot_s),
                       (t.wga_d, t.wga_s), (t.wup_d, t.wup_s),
                       (t.wdn_d, t.wdn_s)):
        rows, cols = src_d.shape
        for kr in range(rows // 128):
            for kc in range(cols // 2048):
                jobs.append((src_d, dst, kr, kc))
    for b0 in range(0, len(jobs), 4):
        batch = []
        for sl, (src_d, dst, kr, kc) in enumerate(jobs[b0:b0 + 4]):
            stg = t.va[:, sl * 2048:(sl + 1) * 2048]
            host = t.lnH if sl < 2 else t.lnL
            base = (sl % 2) * 4096
            pair = host[:, base:base + 4096]
            hiv = bass.AP(host, pair.offset, [list(pair.ap[0]), [1024, 4], [1, 512]])
            lov = bass.AP(host, pair.offset + 512, [list(pair.ap[0]), [1024, 4], [1, 512]])
            sgv = bass.AP(t.va, stg.offset, [list(stg.ap[0]), [512, 4], [1, 512]])
            kin = ["va:%d" % sl]
            kh = "lnH" if sl < 2 else "lnL"
            kpair = ["%s:%d" % (kh, base + j * 128) for j in range(32)]
            rows, cols = src_d.shape
            off = kr * 128 * 2 * cols + kc * 4096
            dap = bass.AP(dst, off, [[2 * cols, 128], [1, 4096]])
            sap = src_d[kr * 128:(kr + 1) * 128, kc * 2048:(kc + 1) * 2048]
            batch.append((sl, stg, sap, hiv, lov, sgv, kin, kpair, dap, pair, dst))
        for sl, stg, sap, hiv, lov, sgv, kin, kpair, dap, pair, dst in batch:
            p.op("sync", lambda E, stg=stg, sap=sap: E.dma_start(stg, sap),
                 writes=kin, sem="qsi%d" % sl)
        for sl, stg, sap, hiv, lov, sgv, kin, kpair, dap, pair, dst in batch:
            p.op("scalar", lambda E, hiv=hiv, sgv=sgv: E.copy(hiv, sgv),
                 reads=kin, writes=kpair[:16])
        for sl, stg, sap, hiv, lov, sgv, kin, kpair, dap, pair, dst in batch:
            p.op("vector", lambda E, lov=lov, sgv=sgv, hiv=hiv: E.tensor_tensor(
                lov, sgv, hiv, op=ALU.subtract),
                 reads=kin + kpair[:1], writes=kpair[16:])
        for sl, stg, sap, hiv, lov, sgv, kin, kpair, dap, pair, dst in batch:
            oeng = "scalar" if sl < 2 else "gpsimd"
            p.op(oeng, lambda E, dap=dap, pair=pair: E.dma_start(dap, pair),
                 reads=kpair, writes=["sW:%s" % dst.name], sem="qsp%d" % sl)



def _layernorm(p, t, src, src_keys, dst, dst_keys, affine=False):
    for j in range(4):
        sj = bass.AP(src.tensor, src.offset + j * 512, [list(src.ap[0]), [1, 512]])
        p.op("vector", lambda E, sj=sj, j=j: E.bn_stats(t.st6[:, j * 6:(j + 1) * 6], sj),
             reads=src_keys, writes=["st6:%d" % j])
    p.op("vector", lambda E: E.bn_aggr(t.ag2[:, :], t.st6[:, :]),
         reads=["st6:%d" % j for j in range(4)], writes=["ag2"])
    p.op("scalar", lambda E: E.activation(t.rstd[:, :], t.ag2[:, 1:2], AF.Sqrt,
                                          bias=t.epsv[:, 0:1]),
         reads=["ag2", "epsv"], writes=["rstd"])
    p.op("vector", lambda E: E.reciprocal(t.rstd[:, :], t.rstd[:, :]),
         reads=["rstd"], writes=["rstd"])
    p.op("vector", lambda E: E.tensor_scalar(dst, src, t.ag2[:, 0:1], t.rstd[:, 0:1],
                                             op0=ALU.subtract, op1=ALU.mult),
         reads=list(src_keys) + ["ag2", "rstd"], writes=list(dst_keys))
    if affine:
        p.op("vector", lambda E: E.tensor_tensor(dst, dst, t.gbc[:, :], op=ALU.mult),
             reads=list(dst_keys) + ["gbc"], writes=list(dst_keys))
        p.op("vector", lambda E: E.tensor_tensor(dst, dst, t.bbc[:, :], op=ALU.add),
             reads=list(dst_keys) + ["bbc"], writes=list(dst_keys))


def _transpose_grp(p, t, src, src_keys, c, grp):
    bank = 4 + (grp % 2)
    fns = []
    for j in range(4):
        dc = grp * 4 + j
        sblk = bass.AP(src.tensor, src.offset + dc * 128,
                       [list(src.ap[0]), [1, 128]])
        fns.append(lambda E, sblk=sblk, j=j, bank=bank: E.transpose(
            t.pb[bank][:, j * 128:(j + 1) * 128], sblk, t.ident[:, :]))
    p.op("tensor", fns, reads=list(src_keys) + ["ident"], writes=["pb%d" % bank])
    cols = [((grp * 4 + j) * CH + c) * 128 for j in range(4)]
    hk = ["lnH:%d" % co for co in cols]
    lk = ["lnL:%d" % co for co in cols]

    def evh(E, bank=bank, cols=cols):
        inst = None
        for j, co in enumerate(cols):
            inst = E.copy(t.lnH[:, co:co + 128],
                          t.pb[bank][:, j * 128:(j + 1) * 128])
        return inst

    def evl(E, bank=bank, cols=cols):
        inst = None
        for j, co in enumerate(cols):
            inst = E.tensor_tensor(t.lnL[:, co:co + 128],
                                   t.pb[bank][:, j * 128:(j + 1) * 128],
                                   t.lnH[:, co:co + 128], op=ALU.subtract)
        return inst
    p.op("scalar", evh, reads=["pb%d" % bank], writes=hk)
    p.op("vector", evl, reads=["pb%d" % bank] + hk, writes=lk)


def _transpose_band(p, t, src, src_keys, c):
    for grp in range(4):
        _transpose_grp(p, t, src, src_keys, c, grp)


def _proj_closures(p, t, wsd, colbase, itile, evac, bank_base=0):
    cls = []
    for dq in range(8):
        def go(dq=dq):
            pq = t.par % 2
            t.par += 1
            grp = (colbase + itile * 512) // 512
            dap = bass.AP(wsd, dq * 256 * 4096 + grp * 1024,
                          [[4096, 128], [128 * 4096, 2], [1, 1024]])
            sap = t.wa[:, pq * 2048:(pq + 1) * 2048].rearrange(
                "p (c j) -> p c j", c=2)
            p.op("sync", lambda E, dap=dap, sap=sap: E.dma_start(sap, dap),
                 reads=["sW:%s" % wsd.name],
                 writes=["wa:%d" % pq], sem="qwa%d" % pq)
            fns = []
            rk = ["wa:%d" % pq]
            for dl in range(2):
                dc = dq * 2 + dl
                for tt in range(CH):
                    co = (dc * CH + tt) * 128
                    lhsH = t.lnH[:, co:co + 128]
                    rhsH = t.wa[:, pq * 2048 + dl * 1024: pq * 2048 + dl * 1024 + 512]
                    rhsL = t.wa[:, pq * 2048 + dl * 1024 + 512: pq * 2048 + dl * 1024 + 1024]
                    st = (dc == 0)
                    fns.append(lambda E, lhsH=lhsH, rhsH=rhsH, tt=tt, st=st: E.matmul(
                        t.pb[bank_base + tt][:, :], lhsH, rhsH, start=st, stop=False))
                    fns.append(lambda E, lhsH=lhsH, rhsL=rhsL, tt=tt: E.matmul(
                        t.pb[bank_base + tt][:, :], lhsH, rhsL, start=False, stop=False))
                    rk += ["lnH:%d" % co, "lnL:%d" % co]
            for dl in range(2):
                dc = dq * 2 + dl
                for tt in range(CH):
                    co = (dc * CH + tt) * 128
                    lhsL = t.lnL[:, co:co + 128]
                    rhsH = t.wa[:, pq * 2048 + dl * 1024: pq * 2048 + dl * 1024 + 512]
                    sp_ = (dc == 15)
                    fns.append(lambda E, lhsL=lhsL, rhsH=rhsH, tt=tt, sp_=sp_: E.matmul(
                        t.pb[bank_base + tt][:, :], lhsL, rhsH, start=False, stop=sp_))
            p.op("tensor", fns, reads=rk,
                 writes=["pb%d" % (bank_base + tt) for tt in range(CH)])
        cls.append(go)

    def evacs():
        for tt in range(CH):
            evac(tt, itile, bank_base)
        for _ in range(12):
            if t.weave:
                t.weave.pop(0)()
    cls.append(evacs)
    return cls


def _proj_unit(p, t, wsd, colbase, itile, evac, bank_base=0):
    for c in _proj_closures(p, t, wsd, colbase, itile, evac, bank_base):
        c()



def emit_tile(p, t, k):
    tok0 = k * TILE
    seq_tile = k % TPS

    def xk(c):
        return "x:%d" % c

    def vak(c):
        return "va:%d" % c

    xband = lambda c: t.x_sb[:, c * D:(c + 1) * D]
    vband = lambda c: t.va[:, c * D:(c + 1) * D]

    if seq_tile != 0:
        p.op("gpsimd", lambda E: E.dma_start(t.scr[:, :], t.retc_d[:, :]),
             writes=["scr"], sem="qrc")
    if k != 0:
        for c in range(CH):
            p.op("sync", lambda E, c=c: E.dma_start(
                xband(c), t.x_d[tok0 + c * 128: tok0 + (c + 1) * 128, :]), writes=[xk(c)],
                 sem="qx%d" % c)
    for c in range(CH):
        _layernorm(p, t, xband(c), [xk(c)], vband(c), [vak(c)])
    def evac_v(tt, itile, bb=0):
        sl = t.va[:, tt * D + itile * 512: tt * D + (itile + 1) * 512]
        p.op("scalar", lambda E, sl=sl, b=bb + tt: E.copy(sl, t.pb[b][:, :]),
             reads=["pb%d" % (bb + tt)], writes=[vak(tt)])

    v0 = _proj_closures(p, t, t.wvt_s, 0, 0, evac_v)
    for c in range(CH):
        _transpose_grp(p, t, vband(c), [vak(c)], c, 0)
    for dq in range(8):
        v0[dq]()
        if dq % 2 == 1 and dq < 7:
            for c in range(CH):
                _transpose_grp(p, t, vband(c), [vak(c)], c, (dq + 1) // 2)
    v0[8]()
    for itile in range(1, 4):
        _proj_unit(p, t, t.wvt_s, 0, itile, evac_v, bank_base=4 * (itile % 2))

    def ret_chunk(c):
        seq_start = (seq_tile == 0 and c == 0)
        fns = []
        reads = [vak(c), "a_t", "sp"]
        for b in range(4):
            for hh in range(4):
                h = 4 * b + hh
                vsl = t.va[:, c * D + h * 128: c * D + (h + 1) * 128]
                psl = t.pb[4 + b][:, hh * 128:(hh + 1) * 128]
                asl = t.a_t[:, h * 128:(h + 1) * 128]
                fns.append(lambda E, psl=psl, asl=asl, vsl=vsl, hh=hh, ss=seq_start:
                           E.matmul(psl, asl, vsl, start=(hh == 0),
                                    stop=(hh == 3 and ss)))
                if not seq_start:
                    prev = (t.va[:, (c - 1) * D + h * 128: (c - 1) * D + (h + 1) * 128]
                            if c > 0 else t.scr[:, h * 128:(h + 1) * 128])
                    ssl = t.sp[:, h * 128:(h + 1) * 128]
                    fns.append(lambda E, psl=psl, ssl=ssl, prev=prev, hh=hh:
                               E.matmul(psl, ssl, prev, start=False, stop=(hh == 3)))
        if not seq_start:
            reads.append(vak(c - 1) if c > 0 else "scr")
        p.op("tensor", fns, reads=reads, writes=["pb%d" % (4 + b) for b in range(4)])
        for b in range(4):
            sl = t.va[:, c * D + b * 512: c * D + (b + 1) * 512]
            p.op("scalar", lambda E, sl=sl, b=b: E.copy(sl, t.pb[4 + b][:, :]),
                 reads=["pb%d" % (4 + b)], writes=[vak(c)])
    def ret_carry():
        if seq_tile != TPS - 1:
            p.op("gpsimd", lambda E: E.dma_start(t.retc_d[:, :], vband(3)),
                 reads=[vak(3)], writes=["retcd"], sem="qrc")

    def evac_q(tt, itile, bb=0):
        sl = t.va[:, tt * D + itile * 512: tt * D + (itile + 1) * 512]
        p.op("vector", lambda E, sl=sl, b=bb + tt: E.tensor_tensor(
            sl, t.pb[b][:, :], sl, op=ALU.mult),
             reads=["pb%d" % (bb + tt), vak(tt)], writes=[vak(tt)])

    q0 = _proj_closures(p, t, t.wqt_s, 0, 0, evac_q)
    ret_chunk(0)
    q0[0]()
    q0[1]()
    ret_chunk(1)
    q0[2]()
    q0[3]()
    ret_chunk(2)
    q0[4]()
    q0[5]()
    ret_chunk(3)
    ret_carry()
    q0[6]()
    q0[7]()
    q0[8]()
    for itile in range(1, 4):
        _proj_unit(p, t, t.wqt_s, 0, itile, evac_q, bank_base=4 * (itile % 2))

    for c in range(CH):
        _layernorm(p, t, vband(c), [vak(c)], vband(c), [vak(c)], affine=True)

    def evac_g(tt, itile, bb=0):
        gsl = t.scr[:, tt * 512:(tt + 1) * 512]
        osl = t.va[:, tt * D + itile * 512: tt * D + (itile + 1) * 512]
        p.op("scalar", lambda E, gsl=gsl, b=bb + tt: E.activation(gsl, t.pb[b][:, :], AF.Silu),
             reads=["pb%d" % (bb + tt)], writes=["scr"]),
        p.op("vector", lambda E, osl=osl, gsl=gsl: E.tensor_tensor(
            osl, osl, gsl, op=ALU.mult),
             reads=[vak(tt), "scr"], writes=[vak(tt)])

    for itile in range(4):
        _proj_unit(p, t, t.wgt_s, 0, itile, evac_g, bank_base=4 * (itile % 2))

    def evac_o(tt, itile, bb=0):
        xsl = t.x_sb[:, tt * D + itile * 512: tt * D + (itile + 1) * 512]
        p.op("vector", lambda E, xsl=xsl, b=bb + tt: E.tensor_tensor(
            xsl, t.pb[b][:, :], xsl, op=ALU.add),
             reads=["pb%d" % (bb + tt), xk(tt)], writes=[xk(tt)])

    o0 = _proj_closures(p, t, t.wot_s, 0, 0, evac_o)
    for c in range(CH):
        _transpose_grp(p, t, vband(c), [vak(c)], c, 0)
    for dq in range(8):
        o0[dq]()
        if dq % 2 == 1 and dq < 7:
            for c in range(CH):
                _transpose_grp(p, t, vband(c), [vak(c)], c, (dq + 1) // 2)
    o0[8]()
    for itile in range(1, 4):
        _proj_unit(p, t, t.wot_s, 0, itile, evac_o, bank_base=4 * (itile % 2))

    for c in range(CH):
        _layernorm(p, t, xband(c), [xk(c)], vband(c), [vak(c)])
    for c in range(CH):
        _transpose_band(p, t, vband(c), [vak(c)], c)

    for tp in range(2):
        ts = (2 * tp, 2 * tp + 1)

        def gu(s, ts=ts):
            ig, ist = divmod(s, 4)
            colbase = ig * 2048 + ist * 512
            for dq in range(8):
                pq = t.par % 2
                t.par += 1
                for wsd, wbuf, wk, qs_, bb in (
                        (t.wga_s, t.wa, "wa", "qwa", 0),
                        (t.wup_s, t.wb, "wb", "qwb", 2)):
                    dap = bass.AP(wsd, dq * 256 * 16384 + (colbase // 512) * 1024,
                                  [[16384, 128], [128 * 16384, 2], [1, 1024]])
                    sap = wbuf[:, pq * 2048:(pq + 1) * 2048].rearrange(
                        "p (c j) -> p c j", c=2)
                    p.op("sync", lambda E, dap=dap, sap=sap: E.dma_start(sap, dap),
                         reads=["sW:%s" % wsd.name],
                         writes=["%s:%d" % (wk, pq)], sem="%s%d" % (qs_, pq))
                    fns = []
                    rk = ["%s:%d" % (wk, pq)]
                    for dl in range(2):
                        dc = dq * 2 + dl
                        for tl in range(2):
                            tt_ = ts[tl]
                            co = (dc * CH + tt_) * 128
                            lhsH = t.lnH[:, co:co + 128]
                            rhsH = wbuf[:, pq * 2048 + dl * 1024: pq * 2048 + dl * 1024 + 512]
                            rhsL = wbuf[:, pq * 2048 + dl * 1024 + 512: pq * 2048 + dl * 1024 + 1024]
                            st = (dc == 0)
                            fns.append(lambda E, lhsH=lhsH, rhsH=rhsH, tl=tl, st=st, bb=bb:
                                       E.matmul(t.pb[bb + tl][:, :], lhsH, rhsH,
                                                start=st, stop=False))
                            fns.append(lambda E, lhsH=lhsH, rhsL=rhsL, tl=tl, bb=bb:
                                       E.matmul(t.pb[bb + tl][:, :], lhsH, rhsL,
                                                start=False, stop=False))
                            rk += ["lnH:%d" % co, "lnL:%d" % co]
                    for dl in range(2):
                        dc = dq * 2 + dl
                        for tl in range(2):
                            tt_ = ts[tl]
                            co = (dc * CH + tt_) * 128
                            lhsL = t.lnL[:, co:co + 128]
                            rhsH = wbuf[:, pq * 2048 + dl * 1024: pq * 2048 + dl * 1024 + 512]
                            sp_ = (dc == 15)
                            fns.append(lambda E, lhsL=lhsL, rhsH=rhsH, tl=tl, sp_=sp_, bb=bb:
                                       E.matmul(t.pb[bb + tl][:, :], lhsL, rhsH,
                                                start=False, stop=sp_))
                    p.op("tensor", fns, reads=rk,
                         writes=["pb%d" % (bb + tl) for tl in range(2)])

        def eps(s):
            ig, ist = divmod(s, 4)
            for tl in range(2):
                hsl = t.hg[:, tl * D + ist * 512: tl * D + (ist + 1) * 512]
                hkeys = ["hg:%d" % (tl * DC + ist * 4 + j) for j in range(4)]
                p.op("scalar", lambda E, hsl=hsl, tl=tl: E.activation(
                    hsl, t.pb[tl][:, :], AF.Silu),
                     reads=["pb%d" % tl], writes=hkeys)
                p.op("vector", lambda E, hsl=hsl, tl=tl: E.tensor_tensor(
                    hsl, hsl, t.pb[2 + tl][:, :], op=ALU.mult),
                     reads=["pb%d" % (2 + tl)] + hkeys, writes=hkeys)

        def spars(s):
            ig, ist = divmod(s, 4)
            for tl in range(2):
                _sparsify_sub(p, t, tl, ist)

        def trans(s):
            ig, ist = divmod(s, 4)
            for tl in range(2):
                bank = 4 + tl
                fns = []
                for j in range(4):
                    ic = ist * 4 + j
                    hsl = t.hg[:, tl * D + ic * 128: tl * D + (ic + 1) * 128]
                    fns.append(lambda E, hsl=hsl, j=j, bank=bank: E.transpose(
                        t.pb[bank][:, j * 128:(j + 1) * 128], hsl, t.ident[:, :]))
                bkeys = ["hg:%d" % (tl * DC + ist * 4 + j) for j in range(4)]
                p.op("tensor", fns, reads=bkeys + ["ident"], writes=["pb%d" % bank])
                hco = tl * D + ist * 512
                hk = ["hsH:%d" % (tl * DC + ist * 4 + j) for j in range(4)]
                lk = ["hsL:%d" % (tl * DC + ist * 4 + j) for j in range(4)]
                p.op("scalar", lambda E, bank=bank, hco=hco: E.copy(
                    t.hsH[:, hco:hco + 512], t.pb[bank][:, :]),
                     reads=["pb%d" % bank], writes=hk)
                p.op("vector", lambda E, bank=bank, hco=hco: E.tensor_tensor(
                    t.hsL[:, hco:hco + 512], t.pb[bank][:, :],
                    t.hsH[:, hco:hco + 512], op=ALU.subtract),
                     reads=["pb%d" % bank] + hk, writes=lk)

        def down(s, nh, ts=ts):
            ig, ist = divmod(s, 4)
            if True:
                for icp in range(2):
                    pq = t.dpar % 2
                    t.dpar += 1
                    r0 = ig * 2048 + (ist * 4 + icp * 2) * 128
                    dap = bass.AP(t.wdn_s, r0 * 4096 + nh * 2048,
                                  [[4096, 128], [128 * 4096, 2], [1, 2048]])
                    sap = t.wd[:, pq * 4096:(pq + 1) * 4096].rearrange(
                        "p (c j) -> p c j", c=2)
                    p.op("gpsimd", lambda E, dap=dap, sap=sap: E.dma_start(sap, dap),
                         reads=["sW:%s" % t.wdn_s.name],
                         writes=["wd:%d" % pq], sem="qwd%d" % pq)
                    fns = []
                    rks = ["wd:%d" % pq]
                    for icl2 in range(2):
                        icl = icp * 2 + icl2
                        ic = ist * 4 + icl
                        rks += ["hsH:%d" % ic, "hsL:%d" % ic,
                                "hsH:%d" % (DC + ic), "hsL:%d" % (DC + ic)]
                        for tl in range(2):
                            hco = tl * D + ic * 128
                            lhsH = t.hsH[:, hco:hco + 128]
                            lhsL = t.hsL[:, hco:hco + 128]
                            for nl in range(2):
                                bank = 4 + tl * 2 + nl
                                rhsH = t.wd[:, pq * 4096 + icl2 * 2048 + nl * 1024:
                                            pq * 4096 + icl2 * 2048 + nl * 1024 + 512]
                                rhsL = t.wd[:, pq * 4096 + icl2 * 2048 + nl * 1024 + 512:
                                            pq * 4096 + icl2 * 2048 + nl * 1024 + 1024]
                                fns.append(lambda E, lhsH=lhsH, rhsH=rhsH, bank=bank, icl=icl:
                                           E.matmul(t.pb[bank][:, :], lhsH, rhsH,
                                                    start=(icl == 0), stop=False))
                                fns.append(lambda E, lhsH=lhsH, rhsL=rhsL, bank=bank:
                                           E.matmul(t.pb[bank][:, :], lhsH, rhsL,
                                                    start=False, stop=False))
                            for nl in range(2):
                                bank = 4 + tl * 2 + nl
                                rhsH = t.wd[:, pq * 4096 + icl2 * 2048 + nl * 1024:
                                            pq * 4096 + icl2 * 2048 + nl * 1024 + 512]
                                fns.append(lambda E, lhsL=lhsL, rhsH=rhsH, bank=bank, icl=icl:
                                           E.matmul(t.pb[bank][:, :], lhsL, rhsH,
                                                    start=False, stop=(icl == 3)))
                    p.op("tensor", fns, reads=rks,
                         writes=["pb%d" % (4 + j) for j in range(4)])
                for tl in range(2):
                    tt_ = ts[tl]
                    for nl in range(2):
                        n = nh * 2 + nl
                        bank = 4 + tl * 2 + nl
                        ysl = t.va[:, tt_ * D + n * 512: tt_ * D + (n + 1) * 512]
                        if s == 0:
                            p.op("scalar", lambda E, ysl=ysl, bank=bank: E.copy(
                                ysl, t.pb[bank][:, :]),
                                 reads=["pb%d" % bank], writes=[vak(tt_)])
                        else:
                            p.op("vector", lambda E, ysl=ysl, bank=bank: E.tensor_tensor(
                                ysl, ysl, t.pb[bank][:, :], op=ALU.add),
                                 reads=["pb%d" % bank, vak(tt_)], writes=[vak(tt_)])

        for s in range(16):
            if s == 0:
                gu(0)
                eps(0)
                spars(0)
            trans(s)
            if s + 1 < 16:
                gu(s + 1)
            down(s, 0)
            down(s, 1)
            if s + 1 < 16:
                eps(s + 1)
                spars(s + 1)

    for c in range(CH):
        p.op("vector", lambda E, c=c: E.tensor_tensor(
            vband(c), vband(c), xband(c), op=ALU.add),
             reads=[vak(c), xk(c)], writes=[vak(c)])
        p.op("gpsimd", lambda E, c=c: E.dma_start(
            t.out_d[tok0 + c * 128: tok0 + (c + 1) * 128, :], vband(c)),
             reads=[vak(c)], writes=["out:%d:%d" % (k, c)], sem="qo%d" % c)


def _sparsify_sub(p, t, tl, ist):
    col0 = tl * D + ist * 512
    sub = t.hg[:, col0:col0 + 512]
    bkeys = ["hg:%d" % (tl * DC + ist * 4 + j) for j in range(4)]
    p.op("scalar", lambda E: E.activation(t.scr[:, 0:512], sub, AF.Abs),
         reads=bkeys, writes=["scr"])
    a = [_stride4(t.scr, 0, i, 128) for i in range(4)]
    w0, w1 = t.wt[:, 0:128], t.wt[:, 128:256]
    acc = [t.scg[:, i * 128:(i + 1) * 128] for i in range(4)]
    ak = ["scg:%d" % i for i in range(4)]

    def tt(dst, x, y, op, dk, rk):
        p.op("vector", lambda E, dst=dst, x=x, y=y, op=op: E.tensor_tensor(
            dst, x, y, op=op), reads=rk, writes=dk)

    tt(w0, a[0], a[1], ALU.is_ge, ["wt0"], ["scr"])
    p.op("vector", lambda E: E.tensor_copy(acc[0], w0), reads=["wt0"], writes=[ak[0]])
    p.op("vector", lambda E: E.tensor_scalar(acc[1], w0, -1.0, None, op0=ALU.mult),
         reads=["wt0"], writes=[ak[1]])
    tt(w1, a[0], a[2], ALU.is_ge, ["wt1"], ["scr"])
    tt(acc[0], acc[0], w1, ALU.add, [ak[0]], [ak[0], "wt1"])
    p.op("vector", lambda E: E.tensor_scalar(acc[2], w1, -1.0, None, op0=ALU.mult),
         reads=["wt1"], writes=[ak[2]])
    tt(w0, a[0], a[3], ALU.is_ge, ["wt0"], ["scr"])
    tt(acc[0], acc[0], w0, ALU.add, [ak[0]], [ak[0], "wt0"])
    p.op("vector", lambda E: E.tensor_copy(acc[3], w0), reads=["wt0"], writes=[ak[3]])
    tt(w1, a[1], a[2], ALU.is_ge, ["wt1"], ["scr"])
    tt(acc[1], acc[1], w1, ALU.add, [ak[1]], [ak[1], "wt1"])
    tt(acc[2], acc[2], w1, ALU.subtract, [ak[2]], [ak[2], "wt1"])
    tt(w0, a[1], a[3], ALU.is_ge, ["wt0"], ["scr"])
    tt(acc[1], acc[1], w0, ALU.add, [ak[1]], [ak[1], "wt0"])
    tt(acc[3], acc[3], w0, ALU.add, [ak[3]], [ak[3], "wt0"])
    tt(w1, a[2], a[3], ALU.is_ge, ["wt1"], ["scr"])
    tt(acc[2], acc[2], w1, ALU.add, [ak[2]], [ak[2], "wt1"])
    tt(acc[3], acc[3], w1, ALU.add, [ak[3]], [ak[3], "wt1"])
    specs = [(0, ALU.is_ge, 2.0), (1, ALU.is_ge, 1.0), (2, ALU.is_ge, 0.0),
             (3, ALU.is_le, 1.0)]
    for i, cmp, thr in specs:
        hi = _stride4(t.hg, col0, i, 128)
        p.op("vector", lambda E, hi=hi, ai=acc[i], cmp=cmp, thr=thr:
             E.scalar_tensor_tensor(hi, ai, thr, hi, op0=cmp, op1=ALU.mult),
             reads=[ak[i]] + bkeys, writes=bkeys)



def _host_inputs(x_shard, ins):
    m = {
        "x": np.ascontiguousarray(np.asarray(x_shard, np.float32).reshape(TOK, D)),
        "wqt": np.ascontiguousarray(np.asarray(ins["L0_attn_Wq"], np.float32).T),
        "wvt": np.ascontiguousarray(np.asarray(ins["L0_attn_Wv"], np.float32).T),
        "wgt": np.ascontiguousarray(np.asarray(ins["L0_attn_Wg"], np.float32).T),
        "wot": np.ascontiguousarray(np.asarray(ins["L0_attn_Wo"], np.float32).T),
        "wgate": np.ascontiguousarray(np.asarray(ins["L0_ffn_w_gate"], np.float32)),
        "wup": np.ascontiguousarray(np.asarray(ins["L0_ffn_w_up"], np.float32)),
        "wdown": np.ascontiguousarray(np.asarray(ins["L0_ffn_w_down"], np.float32)),
        "c_ident": np.eye(128, dtype=np.float32),
        "c_dts": (np.arange(128, dtype=np.float32)[None, :]
                  - np.arange(128, dtype=np.float32)[:, None]),
        "c_iotap1": np.broadcast_to(np.arange(1, 129, dtype=np.float32)[None, :],
                                    (128, 128)).copy(),
        "c_maskut": np.triu(np.ones((128, 128), dtype=np.float32)),
        "c_oh127": (np.arange(128) == 127).astype(np.float32)[:, None].copy(),
        "c_ones": np.ones((1, 128), dtype=np.float32),
        "c_decay_beta": np.asarray(ins["L0_attn_beta"], np.float32).reshape(1, NH),
        "c_lngamma": np.asarray(ins["L0_attn_ln_gamma"], np.float32).reshape(1, D),
        "c_lnbeta": np.asarray(ins["L0_attn_ln_beta"], np.float32).reshape(1, D),
    }
    return m


def kernel(**inputs):
    x = np.asarray(inputs["L0_attn_x"], np.float32)
    nc = build_nc()
    per = B // NCORES
    shared = _host_inputs(x[0:per], inputs)
    in_maps = []
    for i in range(NCORES):
        m = dict(shared)
        m["x"] = np.ascontiguousarray(x[i * per:(i + 1) * per].reshape(TOK, D))
        in_maps.append(m)
    res = run_bass_kernel_spmd(nc, in_maps, core_ids=list(range(NCORES)))
    shards = [np.asarray(r["out"], np.float32).reshape(per, L, D) for r in res.results]
    return np.concatenate(shards, axis=0)


# version 16
# speedup vs baseline: 1.1716x; 1.1716x over previous
import contextlib
import numpy as np
import concourse.bass as bass
import concourse.mybir as mybir
from concourse.bass_utils import run_bass_kernel_spmd

F32 = mybir.dt.float32
BF16 = mybir.dt.bfloat16
AF = mybir.ActivationFunctionType
ALU = mybir.AluOpType

B, L, D, I, NH = 16, 2048, 2048, 8192, 16
NCORES = 8
TOK = (B // NCORES) * L
TILE = 512
CH = TILE // 128
DC = D // 128
NT = TOK // TILE
TPS = L // TILE
LN_EPS = 1e-5
LN2 = float(np.log(2.0))
TINY = float(np.finfo(np.float32).tiny)

ENG_SEM = {"tensor": "te", "vector": "ve", "scalar": "se"}
DMA_SEMS = (["qc", "qwa0", "qwa1", "qwb0", "qwb1", "qwd0", "qwd1", "qrc"]
            + ["qsi%d" % c for c in range(4)] + ["qsp%d" % c for c in range(4)]
            + ["qx%d" % c for c in range(4)] + ["qo%d" % c for c in range(4)])
ALL_ENGS = ("tensor", "vector", "scalar", "sync", "gpsimd")


def _bc(ap, pos, n):
    dims = [list(d) for d in ap.ap]
    dims.insert(pos, [0, n])
    return bass.AP(ap.tensor, ap.offset, dims)


def _stride4(t, col0, i, cnt):
    base = t[:, :]
    return bass.AP(t, base.offset + col0 + i, [list(base.ap[0]), [4, cnt]])


def _flat(dram, k, width):
    return bass.AP(dram, k * 128 * width, [[width, 128], [1, width]])


class Prog:

    def __init__(self):
        self.stream = {e: [] for e in ALL_ENGS}
        self.cnt = {s: 0 for s in list(ENG_SEM.values()) + DMA_SEMS}
        self.last_w = {}
        self.readers = {}
        self.mark = {}

    def op(self, eng, fns, reads=(), writes=(), sem=None):
        if not isinstance(fns, (list, tuple)):
            fns = [fns]
        if sem is None:
            sem = ENG_SEM[eng]
        inc = 16 if sem in DMA_SEMS else 1
        deps = []
        for r in reads:
            lw = self.last_w.get(r)
            if lw:
                deps.append(lw)
        for w in writes:
            lw = self.last_w.get(w)
            if lw:
                deps.append(lw)
            deps.extend(self.readers.get(w, ()))
        for dsem, dval in deps:
            if dsem == sem and eng == "tensor":
                continue
            if dsem in DMA_SEMS:
                dval = self.cnt[dsem]
            key = (eng, dsem)
            if self.mark.get(key, 0) < dval:
                self.mark[key] = dval
                self.stream[eng].append(("wait", dsem, dval))
        self.stream[eng].append(("ops", fns, sem, inc))
        self.cnt[sem] += inc
        pt = (sem, self.cnt[sem])
        for w in writes:
            self.last_w[w] = pt
            self.readers[w] = []
        for r in reads:
            self.readers.setdefault(r, []).append(pt)

    def materialize(self, nc, sems):
        with nc.Block() as blk:
            for eng in ALL_ENGS:
                entries = self.stream[eng]

                def make(entries):
                    def body(E):
                        for ent in entries:
                            if ent[0] == "wait":
                                E.wait_ge(sems[ent[1]], ent[2])
                            else:
                                inst = None
                                for f in ent[1]:
                                    inst = f(E)
                                inst.then_inc(sems[ent[2]], ent[3])
                    return body

                getattr(blk, eng)(make(entries))


class NS:
    pass


def build_nc(nt=NT):
    nc = bass.Bass("TRN2", target_bir_lowering=False)
    t = NS()
    t.x_d = nc.dram_tensor("x", [TOK, D], F32, kind="ExternalInput")
    t.out_d = nc.dram_tensor("out", [TOK, D], F32, kind="ExternalOutput")
    t.wqt_d = nc.dram_tensor("wqt", [D, D], F32, kind="ExternalInput")
    t.wvt_d = nc.dram_tensor("wvt", [D, D], F32, kind="ExternalInput")
    t.wgt_d = nc.dram_tensor("wgt", [D, D], F32, kind="ExternalInput")
    t.wot_d = nc.dram_tensor("wot", [D, D], F32, kind="ExternalInput")
    t.wga_d = nc.dram_tensor("wgate", [D, I], F32, kind="ExternalInput")
    t.wup_d = nc.dram_tensor("wup", [D, I], F32, kind="ExternalInput")
    t.wdn_d = nc.dram_tensor("wdown", [I, D], F32, kind="ExternalInput")
    for nm, shape in (("c_ident", [128, 128]), ("c_dts", [128, 128]),
                      ("c_iotap1", [128, 128]), ("c_maskut", [128, 128]),
                      ("c_oh127", [128, 1]), ("c_ones", [1, 128]),
                      ("c_decay_beta", [1, NH]), ("c_lngamma", [1, D]),
                      ("c_lnbeta", [1, D])):
        setattr(t, nm, nc.dram_tensor(nm, shape, F32, kind="ExternalInput"))
    for nm, srcd in (("wqt", t.wqt_d), ("wvt", t.wvt_d), ("wgt", t.wgt_d),
                     ("wot", t.wot_d), ("wga", t.wga_d), ("wup", t.wup_d),
                     ("wdn", t.wdn_d)):
        r, c = srcd.shape
        setattr(t, nm + "_s", nc.dram_tensor("s_%s" % nm, [r, 2 * c], BF16))
    t.retc_d = nc.dram_tensor("s_retc", [128, D], F32)

    cm = contextlib.ExitStack()

    def sb(name, shape, dt=F32):
        return cm.enter_context(nc.sbuf_tensor(name, shape, dt))

    with cm:
        t.x_sb = sb("x_sb", [128, CH * D])
        t.va = sb("va", [128, CH * D])
        t.lnH = sb("lnH", [128, CH * D], BF16)
        t.lnL = sb("lnL", [128, CH * D], BF16)
        t.hg = sb("hg", [128, 2 * D])
        t.hsH = sb("hsH", [128, 2 * D], BF16)
        t.hsL = sb("hsL", [128, 2 * D], BF16)
        t.wa = sb("wa", [128, 2 * 2048], BF16)
        t.wb = sb("wb", [128, 2 * 2048], BF16)
        t.wd = sb("wd", [128, 2 * 4096], BF16)
        t.scr = sb("scr", [128, D])
        t.scg = sb("scg", [128, 512])
        t.wt = sb("wt", [128, 256])
        t.a_t = sb("a_t", [128, NH * 128])
        t.sp = sb("sp", [128, NH * 128])
        t.gbc = sb("gbc", [128, D])
        t.bbc = sb("bbc", [128, D])
        t.ident = sb("ident", [128, 128])
        t.oh127 = sb("oh127", [128, 1])
        t.ones1 = sb("ones1", [1, 128])
        t.lamv = sb("lamv", [1, NH])
        t.lnl = sb("lnl", [1, NH])
        t.lnl_bc = sb("lnl_bc", [128, NH])
        t.st6 = sb("st6", [128, 24])
        t.ag2 = sb("ag2", [128, 2])
        t.rstd = sb("rstd", [128, 1])
        t.epsv = sb("epsv", [128, 1])
        t.pb = [cm.enter_context(nc.psum_tensor("pb%d" % i, [128, 512], F32))
                for i in range(8)]

        sems = {s: cm.enter_context(nc.semaphore(s))
                for s in list(ENG_SEM.values()) + DMA_SEMS}

        p = Prog()
        t.par = 0
        t.weave = []
        t.dpar = 0
        emit_setup(p, t)
        for c in range(CH):
            p.op("sync", lambda E, c=c: E.dma_start(
                t.x_sb[:, c * D:(c + 1) * D], t.x_d[c * 128:(c + 1) * 128, :]),
                 writes=["x:%d" % c], sem="qx%d" % c)
        emit_weight_split(p, t)
        for k in range(nt):
            emit_tile(p, t, k)
        for c in range(4):
            p.stream["gpsimd"].append(("wait", "qo%d" % c, p.cnt["qo%d" % c]))
        p.materialize(nc, sems)
    return nc



def emit_setup(p, t):
    dts = t.hg[:, 0:128]
    iop1 = t.hg[:, 128:256]
    maskut = t.hg[:, 256:384]
    loads = [(t.ident[:, :], t.c_ident, "ident"), (dts, t.c_dts, "hg:0"),
             (iop1, t.c_iotap1, "hg:1"), (maskut, t.c_maskut, "hg:2"),
             (t.oh127[:, :], t.c_oh127, "oh127"), (t.ones1[:, :], t.c_ones, "ones1"),
             (t.lamv[:, :], t.c_decay_beta, "lamv")]
    for s, d, key in loads:
        p.op("sync", (lambda E, s=s, d=d: E.dma_start(s, d[:, :])), writes=[key],
             sem="qc")
    p.op("sync", lambda E: E.dma_start(t.scr[0:1, :], t.c_lngamma[:, :]), writes=["scr"],
         sem="qc")

    p.op("vector", lambda E: E.memset(t.epsv[:, :], LN_EPS), writes=["epsv"])
    p.op("scalar", lambda E: E.activation(t.lamv[:, :], t.lamv[:, :], AF.Exp, scale=-LN2),
         reads=["lamv"], writes=["lamv"])
    p.op("vector", lambda E: E.tensor_scalar(t.lamv[:, :], t.lamv[:, :], 1.0, -1.0,
                                             op0=ALU.subtract, op1=ALU.mult),
         reads=["lamv"], writes=["lamv"])
    p.op("vector", lambda E: E.tensor_scalar(t.lamv[:, :], t.lamv[:, :], TINY, 1.0 - 1e-9,
                                             op0=ALU.max, op1=ALU.min),
         reads=["lamv"], writes=["lamv"])
    p.op("scalar", lambda E: E.activation(t.lnl[:, :], t.lamv[:, :], AF.Ln),
         reads=["lamv"], writes=["lnl"])

    p.op("tensor", lambda E: E.matmul(t.pb[0][:, 0:NH], t.ones1[0:1, :], t.lnl[0:1, :],
                                      start=True, stop=True),
         reads=["lnl", "ones1"], writes=["pb0"])
    p.op("vector", lambda E: E.tensor_copy(t.lnl_bc[:, :], t.pb[0][:, 0:NH]),
         reads=["pb0"], writes=["lnl_bc"])
    for j in range(4):
        s0, s1 = j * 512, (j + 1) * 512
        p.op("tensor", lambda E, s0=s0, s1=s1: E.matmul(
            t.pb[1][:, :], t.ones1[0:1, :], t.scr[0:1, s0:s1], start=True, stop=True),
             reads=["scr", "ones1"], writes=["pb1"])
        p.op("vector", lambda E, s0=s0, s1=s1: E.tensor_copy(t.gbc[:, s0:s1], t.pb[1][:, :]),
             reads=["pb1"], writes=["gbc"])
    p.op("sync", lambda E: E.dma_start(t.scr[0:1, :], t.c_lnbeta[:, :]), writes=["scr"],
         sem="qc")
    for j in range(4):
        s0, s1 = j * 512, (j + 1) * 512
        p.op("tensor", lambda E, s0=s0, s1=s1: E.matmul(
            t.pb[2][:, :], t.ones1[0:1, :], t.scr[0:1, s0:s1], start=True, stop=True),
             reads=["scr", "ones1"], writes=["pb2"])
        p.op("vector", lambda E, s0=s0, s1=s1: E.tensor_copy(t.bbc[:, s0:s1], t.pb[2][:, :]),
             reads=["pb2"], writes=["bbc"])

    a3 = t.a_t[:, :].rearrange("p (h u) -> p h u", h=NH)
    s3 = t.sp[:, :].rearrange("p (h u) -> p h u", h=NH)
    lnl3 = _bc(t.lnl_bc[:, :], 2, 128)
    p.op("vector", lambda E, dts=dts: E.tensor_tensor(a3, _bc(dts, 1, NH), lnl3, op=ALU.mult),
         reads=["hg:0", "lnl_bc"], writes=["a_t"])
    p.op("scalar", lambda E: E.activation(t.a_t[:, :], t.a_t[:, :], AF.Exp),
         reads=["a_t"], writes=["a_t"])
    p.op("vector", lambda E, maskut=maskut: E.tensor_tensor(a3, a3, _bc(maskut, 1, NH), op=ALU.mult),
         reads=["a_t", "hg:2"], writes=["a_t"])
    p.op("vector", lambda E, iop1=iop1: E.tensor_tensor(s3, _bc(iop1, 1, NH), lnl3, op=ALU.mult),
         reads=["hg:1", "lnl_bc"], writes=["sp"])
    p.op("scalar", lambda E: E.activation(t.sp[:, :], t.sp[:, :], AF.Exp),
         reads=["sp"], writes=["sp"])
    oh3 = bass.AP(t.oh127, 0, [list(t.oh127[:, :].ap[0]), [0, NH], [0, 128]])
    p.op("vector", lambda E: E.tensor_tensor(s3, s3, oh3, op=ALU.mult),
         reads=["sp", "oh127"], writes=["sp"])


def emit_weight_split(p, t):
    jobs = []
    for src_d, dst in ((t.wqt_d, t.wqt_s), (t.wvt_d, t.wvt_s),
                       (t.wgt_d, t.wgt_s), (t.wot_d, t.wot_s),
                       (t.wga_d, t.wga_s), (t.wup_d, t.wup_s),
                       (t.wdn_d, t.wdn_s)):
        rows, cols = src_d.shape
        for kr in range(rows // 128):
            for kc in range(cols // 2048):
                jobs.append((src_d, dst, kr, kc))
    for b0 in range(0, len(jobs), 4):
        batch = []
        for sl, (src_d, dst, kr, kc) in enumerate(jobs[b0:b0 + 4]):
            stg = t.va[:, sl * 2048:(sl + 1) * 2048]
            host = t.lnH if sl < 2 else t.lnL
            base = (sl % 2) * 4096
            pair = host[:, base:base + 4096]
            hiv = bass.AP(host, pair.offset, [list(pair.ap[0]), [1024, 4], [1, 512]])
            lov = bass.AP(host, pair.offset + 512, [list(pair.ap[0]), [1024, 4], [1, 512]])
            sgv = bass.AP(t.va, stg.offset, [list(stg.ap[0]), [512, 4], [1, 512]])
            kin = ["va:%d" % sl]
            kh = "lnH" if sl < 2 else "lnL"
            kpair = ["%s:%d" % (kh, base + j * 128) for j in range(32)]
            rows, cols = src_d.shape
            off = kr * 128 * 2 * cols + kc * 4096
            dap = bass.AP(dst, off, [[2 * cols, 128], [1, 4096]])
            sap = src_d[kr * 128:(kr + 1) * 128, kc * 2048:(kc + 1) * 2048]
            batch.append((sl, stg, sap, hiv, lov, sgv, kin, kpair, dap, pair, dst))
        for sl, stg, sap, hiv, lov, sgv, kin, kpair, dap, pair, dst in batch:
            p.op("sync", lambda E, stg=stg, sap=sap: E.dma_start(stg, sap),
                 writes=kin, sem="qsi%d" % sl)
        for sl, stg, sap, hiv, lov, sgv, kin, kpair, dap, pair, dst in batch:
            p.op("scalar", lambda E, hiv=hiv, sgv=sgv: E.copy(hiv, sgv),
                 reads=kin, writes=kpair[:16])
        for sl, stg, sap, hiv, lov, sgv, kin, kpair, dap, pair, dst in batch:
            p.op("vector", lambda E, lov=lov, sgv=sgv, hiv=hiv: E.tensor_tensor(
                lov, sgv, hiv, op=ALU.subtract),
                 reads=kin + kpair[:1], writes=kpair[16:])
        for sl, stg, sap, hiv, lov, sgv, kin, kpair, dap, pair, dst in batch:
            oeng = "scalar" if sl < 2 else "gpsimd"
            p.op(oeng, lambda E, dap=dap, pair=pair: E.dma_start(dap, pair),
                 reads=kpair, writes=["sW:%s" % dst.name], sem="qsp%d" % sl)



def _layernorm(p, t, src, src_keys, dst, dst_keys, affine=False):
    for j in range(4):
        sj = bass.AP(src.tensor, src.offset + j * 512, [list(src.ap[0]), [1, 512]])
        p.op("vector", lambda E, sj=sj, j=j: E.bn_stats(t.st6[:, j * 6:(j + 1) * 6], sj),
             reads=src_keys, writes=["st6:%d" % j])
    p.op("vector", lambda E: E.bn_aggr(t.ag2[:, :], t.st6[:, :]),
         reads=["st6:%d" % j for j in range(4)], writes=["ag2"])
    p.op("scalar", lambda E: E.activation(t.rstd[:, :], t.ag2[:, 1:2], AF.Sqrt,
                                          bias=t.epsv[:, 0:1]),
         reads=["ag2", "epsv"], writes=["rstd"])
    p.op("vector", lambda E: E.reciprocal(t.rstd[:, :], t.rstd[:, :]),
         reads=["rstd"], writes=["rstd"])
    p.op("vector", lambda E: E.tensor_scalar(dst, src, t.ag2[:, 0:1], t.rstd[:, 0:1],
                                             op0=ALU.subtract, op1=ALU.mult),
         reads=list(src_keys) + ["ag2", "rstd"], writes=list(dst_keys))
    if affine:
        p.op("vector", lambda E: E.tensor_tensor(dst, dst, t.gbc[:, :], op=ALU.mult),
             reads=list(dst_keys) + ["gbc"], writes=list(dst_keys))
        p.op("vector", lambda E: E.tensor_tensor(dst, dst, t.bbc[:, :], op=ALU.add),
             reads=list(dst_keys) + ["bbc"], writes=list(dst_keys))


def _transpose_grp(p, t, src, src_keys, c, grp):
    bank = 4 + (grp % 2)
    fns = []
    for j in range(4):
        dc = grp * 4 + j
        sblk = bass.AP(src.tensor, src.offset + dc * 128,
                       [list(src.ap[0]), [1, 128]])
        fns.append(lambda E, sblk=sblk, j=j, bank=bank: E.transpose(
            t.pb[bank][:, j * 128:(j + 1) * 128], sblk, t.ident[:, :]))
    p.op("tensor", fns, reads=list(src_keys) + ["ident"], writes=["pb%d" % bank])
    cols = [((grp * 4 + j) * CH + c) * 128 for j in range(4)]
    hk = ["lnH:%d" % co for co in cols]
    lk = ["lnL:%d" % co for co in cols]

    def evh(E, bank=bank, cols=cols):
        inst = None
        for j, co in enumerate(cols):
            inst = E.copy(t.lnH[:, co:co + 128],
                          t.pb[bank][:, j * 128:(j + 1) * 128])
        return inst

    def evl(E, bank=bank, cols=cols):
        inst = None
        for j, co in enumerate(cols):
            inst = E.tensor_tensor(t.lnL[:, co:co + 128],
                                   t.pb[bank][:, j * 128:(j + 1) * 128],
                                   t.lnH[:, co:co + 128], op=ALU.subtract)
        return inst
    p.op("scalar", evh, reads=["pb%d" % bank], writes=hk)
    p.op("vector", evl, reads=["pb%d" % bank] + hk, writes=lk)


def _transpose_band(p, t, src, src_keys, c):
    for grp in range(4):
        _transpose_grp(p, t, src, src_keys, c, grp)


def _proj_closures(p, t, wsd, colbase, itile, evac, bank_base=0):
    cls = []
    for dq in range(8):
        def go(dq=dq):
            pq = t.par % 2
            t.par += 1
            grp = (colbase + itile * 512) // 512
            dap = bass.AP(wsd, dq * 256 * 4096 + grp * 1024,
                          [[4096, 128], [128 * 4096, 2], [1, 1024]])
            sap = t.wa[:, pq * 2048:(pq + 1) * 2048].rearrange(
                "p (c j) -> p c j", c=2)
            p.op("sync", lambda E, dap=dap, sap=sap: E.dma_start(sap, dap),
                 reads=["sW:%s" % wsd.name],
                 writes=["wa:%d" % pq], sem="qwa%d" % pq)
            fns = []
            rk = ["wa:%d" % pq]
            for dl in range(2):
                dc = dq * 2 + dl
                for tt in range(CH):
                    co = (dc * CH + tt) * 128
                    lhsH = t.lnH[:, co:co + 128]
                    rhsH = t.wa[:, pq * 2048 + dl * 1024: pq * 2048 + dl * 1024 + 512]
                    rhsL = t.wa[:, pq * 2048 + dl * 1024 + 512: pq * 2048 + dl * 1024 + 1024]
                    st = (dc == 0)
                    fns.append(lambda E, lhsH=lhsH, rhsH=rhsH, tt=tt, st=st: E.matmul(
                        t.pb[bank_base + tt][:, :], lhsH, rhsH, start=st, stop=False))
                    fns.append(lambda E, lhsH=lhsH, rhsL=rhsL, tt=tt: E.matmul(
                        t.pb[bank_base + tt][:, :], lhsH, rhsL, start=False, stop=False))
                    rk += ["lnH:%d" % co, "lnL:%d" % co]
            for dl in range(2):
                dc = dq * 2 + dl
                for tt in range(CH):
                    co = (dc * CH + tt) * 128
                    lhsL = t.lnL[:, co:co + 128]
                    rhsH = t.wa[:, pq * 2048 + dl * 1024: pq * 2048 + dl * 1024 + 512]
                    sp_ = (dc == 15)
                    fns.append(lambda E, lhsL=lhsL, rhsH=rhsH, tt=tt, sp_=sp_: E.matmul(
                        t.pb[bank_base + tt][:, :], lhsL, rhsH, start=False, stop=sp_))
            p.op("tensor", fns, reads=rk,
                 writes=["pb%d" % (bank_base + tt) for tt in range(CH)])
        cls.append(go)

    def evacs():
        for tt in range(CH):
            evac(tt, itile, bank_base)
        for _ in range(12):
            if t.weave:
                t.weave.pop(0)()
    cls.append(evacs)
    return cls


def _proj_unit(p, t, wsd, colbase, itile, evac, bank_base=0):
    for c in _proj_closures(p, t, wsd, colbase, itile, evac, bank_base):
        c()



def emit_tile(p, t, k):
    tok0 = k * TILE
    seq_tile = k % TPS

    def xk(c):
        return "x:%d" % c

    def vak(c):
        return "va:%d" % c

    xband = lambda c: t.x_sb[:, c * D:(c + 1) * D]
    vband = lambda c: t.va[:, c * D:(c + 1) * D]

    if seq_tile != 0:
        p.op("gpsimd", lambda E: E.dma_start(t.scr[:, :], t.retc_d[:, :]),
             writes=["scr"], sem="qrc")
    if k != 0:
        for c in range(CH):
            p.op("sync", lambda E, c=c: E.dma_start(
                xband(c), t.x_d[tok0 + c * 128: tok0 + (c + 1) * 128, :]), writes=[xk(c)],
                 sem="qx%d" % c)
    for c in range(CH):
        _layernorm(p, t, xband(c), [xk(c)], vband(c), [vak(c)])
    def evac_v(tt, itile, bb=0):
        sl = t.va[:, tt * D + itile * 512: tt * D + (itile + 1) * 512]
        p.op("scalar", lambda E, sl=sl, b=bb + tt: E.copy(sl, t.pb[b][:, :]),
             reads=["pb%d" % (bb + tt)], writes=[vak(tt)])

    v0 = _proj_closures(p, t, t.wvt_s, 0, 0, evac_v)
    for c in range(CH):
        _transpose_grp(p, t, vband(c), [vak(c)], c, 0)
    for dq in range(8):
        v0[dq]()
        if dq % 2 == 1 and dq < 7:
            for c in range(CH):
                _transpose_grp(p, t, vband(c), [vak(c)], c, (dq + 1) // 2)
    v0[8]()
    for itile in range(1, 4):
        _proj_unit(p, t, t.wvt_s, 0, itile, evac_v, bank_base=4 * (itile % 2))

    def ret_chunk(c):
        seq_start = (seq_tile == 0 and c == 0)
        fns = []
        reads = [vak(c), "a_t", "sp"]
        for b in range(4):
            for hh in range(4):
                h = 4 * b + hh
                vsl = t.va[:, c * D + h * 128: c * D + (h + 1) * 128]
                psl = t.pb[4 + b][:, hh * 128:(hh + 1) * 128]
                asl = t.a_t[:, h * 128:(h + 1) * 128]
                fns.append(lambda E, psl=psl, asl=asl, vsl=vsl, hh=hh, ss=seq_start:
                           E.matmul(psl, asl, vsl, start=(hh == 0),
                                    stop=(hh == 3 and ss)))
                if not seq_start:
                    prev = (t.va[:, (c - 1) * D + h * 128: (c - 1) * D + (h + 1) * 128]
                            if c > 0 else t.scr[:, h * 128:(h + 1) * 128])
                    ssl = t.sp[:, h * 128:(h + 1) * 128]
                    fns.append(lambda E, psl=psl, ssl=ssl, prev=prev, hh=hh:
                               E.matmul(psl, ssl, prev, start=False, stop=(hh == 3)))
        if not seq_start:
            reads.append(vak(c - 1) if c > 0 else "scr")
        p.op("tensor", fns, reads=reads, writes=["pb%d" % (4 + b) for b in range(4)])
        for b in range(4):
            sl = t.va[:, c * D + b * 512: c * D + (b + 1) * 512]
            p.op("scalar", lambda E, sl=sl, b=b: E.copy(sl, t.pb[4 + b][:, :]),
                 reads=["pb%d" % (4 + b)], writes=[vak(c)])
    def ret_carry():
        if seq_tile != TPS - 1:
            p.op("gpsimd", lambda E: E.dma_start(t.retc_d[:, :], vband(3)),
                 reads=[vak(3)], writes=["retcd"], sem="qrc")

    def evac_q(tt, itile, bb=0):
        sl = t.va[:, tt * D + itile * 512: tt * D + (itile + 1) * 512]
        p.op("vector", lambda E, sl=sl, b=bb + tt: E.tensor_tensor(
            sl, t.pb[b][:, :], sl, op=ALU.mult),
             reads=["pb%d" % (bb + tt), vak(tt)], writes=[vak(tt)])

    q0 = _proj_closures(p, t, t.wqt_s, 0, 0, evac_q)
    ret_chunk(0)
    q0[0]()
    q0[1]()
    ret_chunk(1)
    q0[2]()
    q0[3]()
    ret_chunk(2)
    q0[4]()
    q0[5]()
    ret_chunk(3)
    ret_carry()
    q0[6]()
    q0[7]()
    q0[8]()
    for itile in range(1, 4):
        _proj_unit(p, t, t.wqt_s, 0, itile, evac_q, bank_base=4 * (itile % 2))

    for c in range(CH):
        _layernorm(p, t, vband(c), [vak(c)], vband(c), [vak(c)], affine=True)

    def evac_g(tt, itile, bb=0):
        gsl = t.scr[:, tt * 512:(tt + 1) * 512]
        osl = t.va[:, tt * D + itile * 512: tt * D + (itile + 1) * 512]
        p.op("scalar", lambda E, gsl=gsl, b=bb + tt: E.activation(gsl, t.pb[b][:, :], AF.Silu),
             reads=["pb%d" % (bb + tt)], writes=["scr"]),
        p.op("vector", lambda E, osl=osl, gsl=gsl: E.tensor_tensor(
            osl, osl, gsl, op=ALU.mult),
             reads=[vak(tt), "scr"], writes=[vak(tt)])

    for itile in range(4):
        _proj_unit(p, t, t.wgt_s, 0, itile, evac_g, bank_base=4 * (itile % 2))

    def evac_o(tt, itile, bb=0):
        xsl = t.x_sb[:, tt * D + itile * 512: tt * D + (itile + 1) * 512]
        p.op("vector", lambda E, xsl=xsl, b=bb + tt: E.tensor_tensor(
            xsl, t.pb[b][:, :], xsl, op=ALU.add),
             reads=["pb%d" % (bb + tt), xk(tt)], writes=[xk(tt)])

    o0 = _proj_closures(p, t, t.wot_s, 0, 0, evac_o)
    for c in range(CH):
        _transpose_grp(p, t, vband(c), [vak(c)], c, 0)
    for dq in range(8):
        o0[dq]()
        if dq % 2 == 1 and dq < 7:
            for c in range(CH):
                _transpose_grp(p, t, vband(c), [vak(c)], c, (dq + 1) // 2)
    o0[8]()
    for itile in range(1, 4):
        _proj_unit(p, t, t.wot_s, 0, itile, evac_o, bank_base=4 * (itile % 2))

    for c in range(CH):
        _layernorm(p, t, xband(c), [xk(c)], vband(c), [vak(c)])
    for c in range(CH):
        _transpose_band(p, t, vband(c), [vak(c)], c)

    for tp in range(2):
        ts = (2 * tp, 2 * tp + 1)

        def gu(s, half, ts=ts):
            ig, ist = divmod(s, 4)
            colbase = ig * 2048 + ist * 512
            for dq in range(4 * half, 4 * half + 4):
                pq = t.par % 2
                t.par += 1
                for wsd, wbuf, wk, qs_, bb in (
                        (t.wga_s, t.wa, "wa", "qwa", 0),
                        (t.wup_s, t.wb, "wb", "qwb", 2)):
                    dap = bass.AP(wsd, dq * 256 * 16384 + (colbase // 512) * 1024,
                                  [[16384, 128], [128 * 16384, 2], [1, 1024]])
                    sap = wbuf[:, pq * 2048:(pq + 1) * 2048].rearrange(
                        "p (c j) -> p c j", c=2)
                    p.op("sync", lambda E, dap=dap, sap=sap: E.dma_start(sap, dap),
                         reads=["sW:%s" % wsd.name],
                         writes=["%s:%d" % (wk, pq)], sem="%s%d" % (qs_, pq))
                    fns = []
                    rk = ["%s:%d" % (wk, pq)]
                    for dl in range(2):
                        dc = dq * 2 + dl
                        for tl in range(2):
                            tt_ = ts[tl]
                            co = (dc * CH + tt_) * 128
                            lhsH = t.lnH[:, co:co + 128]
                            rhsH = wbuf[:, pq * 2048 + dl * 1024: pq * 2048 + dl * 1024 + 512]
                            rhsL = wbuf[:, pq * 2048 + dl * 1024 + 512: pq * 2048 + dl * 1024 + 1024]
                            st = (dc == 0)
                            fns.append(lambda E, lhsH=lhsH, rhsH=rhsH, tl=tl, st=st, bb=bb:
                                       E.matmul(t.pb[bb + tl][:, :], lhsH, rhsH,
                                                start=st, stop=False))
                            fns.append(lambda E, lhsH=lhsH, rhsL=rhsL, tl=tl, bb=bb:
                                       E.matmul(t.pb[bb + tl][:, :], lhsH, rhsL,
                                                start=False, stop=False))
                            rk += ["lnH:%d" % co, "lnL:%d" % co]
                    for dl in range(2):
                        dc = dq * 2 + dl
                        for tl in range(2):
                            tt_ = ts[tl]
                            co = (dc * CH + tt_) * 128
                            lhsL = t.lnL[:, co:co + 128]
                            rhsH = wbuf[:, pq * 2048 + dl * 1024: pq * 2048 + dl * 1024 + 512]
                            sp_ = (dc == 15)
                            fns.append(lambda E, lhsL=lhsL, rhsH=rhsH, tl=tl, sp_=sp_, bb=bb:
                                       E.matmul(t.pb[bb + tl][:, :], lhsL, rhsH,
                                                start=False, stop=sp_))
                    p.op("tensor", fns, reads=rk,
                         writes=["pb%d" % (bb + tl) for tl in range(2)])

        def eps(s):
            ig, ist = divmod(s, 4)
            for tl in range(2):
                hsl = t.hg[:, tl * D + ist * 512: tl * D + (ist + 1) * 512]
                hkeys = ["hg:%d" % (tl * DC + ist * 4 + j) for j in range(4)]
                p.op("scalar", lambda E, hsl=hsl, tl=tl: E.activation(
                    hsl, t.pb[tl][:, :], AF.Silu),
                     reads=["pb%d" % tl], writes=hkeys)
                p.op("vector", lambda E, hsl=hsl, tl=tl: E.tensor_tensor(
                    hsl, hsl, t.pb[2 + tl][:, :], op=ALU.mult),
                     reads=["pb%d" % (2 + tl)] + hkeys, writes=hkeys)

        def spars(s):
            ig, ist = divmod(s, 4)
            for tl in range(2):
                _sparsify_sub(p, t, tl, ist)

        def trans(s):
            ig, ist = divmod(s, 4)
            for tl in range(2):
                bank = 4 + tl
                fns = []
                for j in range(4):
                    ic = ist * 4 + j
                    hsl = t.hg[:, tl * D + ic * 128: tl * D + (ic + 1) * 128]
                    fns.append(lambda E, hsl=hsl, j=j, bank=bank: E.transpose(
                        t.pb[bank][:, j * 128:(j + 1) * 128], hsl, t.ident[:, :]))
                bkeys = ["hg:%d" % (tl * DC + ist * 4 + j) for j in range(4)]
                p.op("tensor", fns, reads=bkeys + ["ident"], writes=["pb%d" % bank])
                hco = tl * D + ist * 512
                hk = ["hsH:%d" % (tl * DC + ist * 4 + j) for j in range(4)]
                lk = ["hsL:%d" % (tl * DC + ist * 4 + j) for j in range(4)]
                p.op("scalar", lambda E, bank=bank, hco=hco: E.copy(
                    t.hsH[:, hco:hco + 512], t.pb[bank][:, :]),
                     reads=["pb%d" % bank], writes=hk)
                p.op("vector", lambda E, bank=bank, hco=hco: E.tensor_tensor(
                    t.hsL[:, hco:hco + 512], t.pb[bank][:, :],
                    t.hsH[:, hco:hco + 512], op=ALU.subtract),
                     reads=["pb%d" % bank] + hk, writes=lk)

        def down(s, nh, ts=ts):
            ig, ist = divmod(s, 4)
            if True:
                for icp in range(2):
                    pq = t.dpar % 2
                    t.dpar += 1
                    r0 = ig * 2048 + (ist * 4 + icp * 2) * 128
                    dap = bass.AP(t.wdn_s, r0 * 4096 + nh * 2048,
                                  [[4096, 128], [128 * 4096, 2], [1, 2048]])
                    sap = t.wd[:, pq * 4096:(pq + 1) * 4096].rearrange(
                        "p (c j) -> p c j", c=2)
                    p.op("gpsimd", lambda E, dap=dap, sap=sap: E.dma_start(sap, dap),
                         reads=["sW:%s" % t.wdn_s.name],
                         writes=["wd:%d" % pq], sem="qwd%d" % pq)
                    fns = []
                    rks = ["wd:%d" % pq]
                    for icl2 in range(2):
                        icl = icp * 2 + icl2
                        ic = ist * 4 + icl
                        rks += ["hsH:%d" % ic, "hsL:%d" % ic,
                                "hsH:%d" % (DC + ic), "hsL:%d" % (DC + ic)]
                        for tl in range(2):
                            hco = tl * D + ic * 128
                            lhsH = t.hsH[:, hco:hco + 128]
                            lhsL = t.hsL[:, hco:hco + 128]
                            for nl in range(2):
                                bank = 4 + tl * 2 + nl
                                rhsH = t.wd[:, pq * 4096 + icl2 * 2048 + nl * 1024:
                                            pq * 4096 + icl2 * 2048 + nl * 1024 + 512]
                                rhsL = t.wd[:, pq * 4096 + icl2 * 2048 + nl * 1024 + 512:
                                            pq * 4096 + icl2 * 2048 + nl * 1024 + 1024]
                                fns.append(lambda E, lhsH=lhsH, rhsH=rhsH, bank=bank, icl=icl:
                                           E.matmul(t.pb[bank][:, :], lhsH, rhsH,
                                                    start=(icl == 0), stop=False))
                                fns.append(lambda E, lhsH=lhsH, rhsL=rhsL, bank=bank:
                                           E.matmul(t.pb[bank][:, :], lhsH, rhsL,
                                                    start=False, stop=False))
                            for nl in range(2):
                                bank = 4 + tl * 2 + nl
                                rhsH = t.wd[:, pq * 4096 + icl2 * 2048 + nl * 1024:
                                            pq * 4096 + icl2 * 2048 + nl * 1024 + 512]
                                fns.append(lambda E, lhsL=lhsL, rhsH=rhsH, bank=bank, icl=icl:
                                           E.matmul(t.pb[bank][:, :], lhsL, rhsH,
                                                    start=False, stop=(icl == 3)))
                    p.op("tensor", fns, reads=rks,
                         writes=["pb%d" % (4 + j) for j in range(4)])
                for tl in range(2):
                    tt_ = ts[tl]
                    for nl in range(2):
                        n = nh * 2 + nl
                        bank = 4 + tl * 2 + nl
                        ysl = t.va[:, tt_ * D + n * 512: tt_ * D + (n + 1) * 512]
                        if s == 0:
                            p.op("scalar", lambda E, ysl=ysl, bank=bank: E.copy(
                                ysl, t.pb[bank][:, :]),
                                 reads=["pb%d" % bank], writes=[vak(tt_)])
                        else:
                            p.op("vector", lambda E, ysl=ysl, bank=bank: E.tensor_tensor(
                                ysl, ysl, t.pb[bank][:, :], op=ALU.add),
                                 reads=["pb%d" % bank, vak(tt_)], writes=[vak(tt_)])

        for s in range(16):
            if s == 0:
                gu(0, 0)
                gu(0, 1)
                eps(0)
                spars(0)
            if s + 1 < 16:
                gu(s + 1, 0)
            trans(s)
            if s + 1 < 16:
                gu(s + 1, 1)
            down(s, 0)
            down(s, 1)
            if s + 1 < 16:
                eps(s + 1)
                spars(s + 1)

    for c in range(CH):
        p.op("vector", lambda E, c=c: E.tensor_tensor(
            vband(c), vband(c), xband(c), op=ALU.add),
             reads=[vak(c), xk(c)], writes=[vak(c)])
        p.op("gpsimd", lambda E, c=c: E.dma_start(
            t.out_d[tok0 + c * 128: tok0 + (c + 1) * 128, :], vband(c)),
             reads=[vak(c)], writes=["out:%d:%d" % (k, c)], sem="qo%d" % c)


def _sparsify_sub(p, t, tl, ist):
    col0 = tl * D + ist * 512
    sub = t.hg[:, col0:col0 + 512]
    bkeys = ["hg:%d" % (tl * DC + ist * 4 + j) for j in range(4)]
    p.op("scalar", lambda E: E.activation(t.scr[:, 0:512], sub, AF.Abs),
         reads=bkeys, writes=["scr"])
    a = [_stride4(t.scr, 0, i, 128) for i in range(4)]
    w0, w1 = t.wt[:, 0:128], t.wt[:, 128:256]
    acc = [t.scg[:, i * 128:(i + 1) * 128] for i in range(4)]
    ak = ["scg:%d" % i for i in range(4)]

    def tt(dst, x, y, op, dk, rk):
        p.op("vector", lambda E, dst=dst, x=x, y=y, op=op: E.tensor_tensor(
            dst, x, y, op=op), reads=rk, writes=dk)

    tt(w0, a[0], a[1], ALU.is_ge, ["wt0"], ["scr"])
    p.op("vector", lambda E: E.tensor_copy(acc[0], w0), reads=["wt0"], writes=[ak[0]])
    p.op("vector", lambda E: E.tensor_scalar(acc[1], w0, -1.0, None, op0=ALU.mult),
         reads=["wt0"], writes=[ak[1]])
    tt(w1, a[0], a[2], ALU.is_ge, ["wt1"], ["scr"])
    tt(acc[0], acc[0], w1, ALU.add, [ak[0]], [ak[0], "wt1"])
    p.op("vector", lambda E: E.tensor_scalar(acc[2], w1, -1.0, None, op0=ALU.mult),
         reads=["wt1"], writes=[ak[2]])
    tt(w0, a[0], a[3], ALU.is_ge, ["wt0"], ["scr"])
    tt(acc[0], acc[0], w0, ALU.add, [ak[0]], [ak[0], "wt0"])
    p.op("vector", lambda E: E.tensor_copy(acc[3], w0), reads=["wt0"], writes=[ak[3]])
    tt(w1, a[1], a[2], ALU.is_ge, ["wt1"], ["scr"])
    tt(acc[1], acc[1], w1, ALU.add, [ak[1]], [ak[1], "wt1"])
    tt(acc[2], acc[2], w1, ALU.subtract, [ak[2]], [ak[2], "wt1"])
    tt(w0, a[1], a[3], ALU.is_ge, ["wt0"], ["scr"])
    tt(acc[1], acc[1], w0, ALU.add, [ak[1]], [ak[1], "wt0"])
    tt(acc[3], acc[3], w0, ALU.add, [ak[3]], [ak[3], "wt0"])
    tt(w1, a[2], a[3], ALU.is_ge, ["wt1"], ["scr"])
    tt(acc[2], acc[2], w1, ALU.add, [ak[2]], [ak[2], "wt1"])
    tt(acc[3], acc[3], w1, ALU.add, [ak[3]], [ak[3], "wt1"])
    specs = [(0, ALU.is_ge, 2.0), (1, ALU.is_ge, 1.0), (2, ALU.is_ge, 0.0),
             (3, ALU.is_le, 1.0)]
    for i, cmp, thr in specs:
        hi = _stride4(t.hg, col0, i, 128)
        p.op("vector", lambda E, hi=hi, ai=acc[i], cmp=cmp, thr=thr:
             E.scalar_tensor_tensor(hi, ai, thr, hi, op0=cmp, op1=ALU.mult),
             reads=[ak[i]] + bkeys, writes=bkeys)



def _host_inputs(x_shard, ins):
    m = {
        "x": np.ascontiguousarray(np.asarray(x_shard, np.float32).reshape(TOK, D)),
        "wqt": np.ascontiguousarray(np.asarray(ins["L0_attn_Wq"], np.float32).T),
        "wvt": np.ascontiguousarray(np.asarray(ins["L0_attn_Wv"], np.float32).T),
        "wgt": np.ascontiguousarray(np.asarray(ins["L0_attn_Wg"], np.float32).T),
        "wot": np.ascontiguousarray(np.asarray(ins["L0_attn_Wo"], np.float32).T),
        "wgate": np.ascontiguousarray(np.asarray(ins["L0_ffn_w_gate"], np.float32)),
        "wup": np.ascontiguousarray(np.asarray(ins["L0_ffn_w_up"], np.float32)),
        "wdown": np.ascontiguousarray(np.asarray(ins["L0_ffn_w_down"], np.float32)),
        "c_ident": np.eye(128, dtype=np.float32),
        "c_dts": (np.arange(128, dtype=np.float32)[None, :]
                  - np.arange(128, dtype=np.float32)[:, None]),
        "c_iotap1": np.broadcast_to(np.arange(1, 129, dtype=np.float32)[None, :],
                                    (128, 128)).copy(),
        "c_maskut": np.triu(np.ones((128, 128), dtype=np.float32)),
        "c_oh127": (np.arange(128) == 127).astype(np.float32)[:, None].copy(),
        "c_ones": np.ones((1, 128), dtype=np.float32),
        "c_decay_beta": np.asarray(ins["L0_attn_beta"], np.float32).reshape(1, NH),
        "c_lngamma": np.asarray(ins["L0_attn_ln_gamma"], np.float32).reshape(1, D),
        "c_lnbeta": np.asarray(ins["L0_attn_ln_beta"], np.float32).reshape(1, D),
    }
    return m


def kernel(**inputs):
    x = np.asarray(inputs["L0_attn_x"], np.float32)
    nc = build_nc()
    per = B // NCORES
    shared = _host_inputs(x[0:per], inputs)
    in_maps = []
    for i in range(NCORES):
        m = dict(shared)
        m["x"] = np.ascontiguousarray(x[i * per:(i + 1) * per].reshape(TOK, D))
        in_maps.append(m)
    res = run_bass_kernel_spmd(nc, in_maps, core_ids=list(range(NCORES)))
    shards = [np.asarray(r["out"], np.float32).reshape(per, L, D) for r in res.results]
    return np.concatenate(shards, axis=0)


# version 17 (final)
# speedup vs baseline: 1.2335x; 1.0528x over previous
import contextlib
import numpy as np
import concourse.bass as bass
import concourse.mybir as mybir
from concourse.bass_utils import run_bass_kernel_spmd

F32 = mybir.dt.float32
BF16 = mybir.dt.bfloat16
AF = mybir.ActivationFunctionType
ALU = mybir.AluOpType

B, L, D, I, NH = 16, 2048, 2048, 8192, 16
NCORES = 8
TOK = (B // NCORES) * L
TILE = 512
CH = TILE // 128
DC = D // 128
NT = TOK // TILE
TPS = L // TILE
LN_EPS = 1e-5
LN2 = float(np.log(2.0))
TINY = float(np.finfo(np.float32).tiny)

ENG_SEM = {"tensor": "te", "vector": "ve", "scalar": "se"}
DMA_SEMS = (["qc", "qwa0", "qwa1", "qwb0", "qwb1", "qwd0", "qwd1", "qrc"]
            + ["qsi%d" % c for c in range(4)] + ["qsp%d" % c for c in range(4)]
            + ["qx%d" % c for c in range(4)] + ["qo%d" % c for c in range(4)])
ALL_ENGS = ("tensor", "vector", "scalar", "sync", "gpsimd")


def _bc(ap, pos, n):
    dims = [list(d) for d in ap.ap]
    dims.insert(pos, [0, n])
    return bass.AP(ap.tensor, ap.offset, dims)


def _stride4(t, col0, i, cnt):
    base = t[:, :]
    return bass.AP(t, base.offset + col0 + i, [list(base.ap[0]), [4, cnt]])


def _flat(dram, k, width):
    return bass.AP(dram, k * 128 * width, [[width, 128], [1, width]])


class Prog:

    def __init__(self):
        self.stream = {e: [] for e in ALL_ENGS}
        self.cnt = {s: 0 for s in list(ENG_SEM.values()) + DMA_SEMS}
        self.last_w = {}
        self.readers = {}
        self.mark = {}

    def op(self, eng, fns, reads=(), writes=(), sem=None):
        if not isinstance(fns, (list, tuple)):
            fns = [fns]
        if sem is None:
            sem = ENG_SEM[eng]
        inc = 16 if sem in DMA_SEMS else 1
        deps = []
        for r in reads:
            lw = self.last_w.get(r)
            if lw:
                deps.append(lw)
        for w in writes:
            lw = self.last_w.get(w)
            if lw:
                deps.append(lw)
            deps.extend(self.readers.get(w, ()))
        for dsem, dval in deps:
            if dsem == sem and eng == "tensor":
                continue
            if dsem in DMA_SEMS:
                dval = self.cnt[dsem]
            key = (eng, dsem)
            if self.mark.get(key, 0) < dval:
                self.mark[key] = dval
                self.stream[eng].append(("wait", dsem, dval))
        self.stream[eng].append(("ops", fns, sem, inc))
        self.cnt[sem] += inc
        pt = (sem, self.cnt[sem])
        for w in writes:
            self.last_w[w] = pt
            self.readers[w] = []
        for r in reads:
            self.readers.setdefault(r, []).append(pt)

    def materialize(self, nc, sems):
        with nc.Block() as blk:
            for eng in ALL_ENGS:
                entries = self.stream[eng]

                def make(entries):
                    def body(E):
                        for ent in entries:
                            if ent[0] == "wait":
                                E.wait_ge(sems[ent[1]], ent[2])
                            else:
                                inst = None
                                for f in ent[1]:
                                    inst = f(E)
                                inst.then_inc(sems[ent[2]], ent[3])
                    return body

                getattr(blk, eng)(make(entries))


class NS:
    pass


def build_nc(nt=NT):
    nc = bass.Bass("TRN2", target_bir_lowering=False)
    t = NS()
    t.x_d = nc.dram_tensor("x", [TOK, D], F32, kind="ExternalInput")
    t.out_d = nc.dram_tensor("out", [TOK, D], F32, kind="ExternalOutput")
    t.wqt_d = nc.dram_tensor("wqt", [D, D], F32, kind="ExternalInput")
    t.wvt_d = nc.dram_tensor("wvt", [D, D], F32, kind="ExternalInput")
    t.wgt_d = nc.dram_tensor("wgt", [D, D], F32, kind="ExternalInput")
    t.wot_d = nc.dram_tensor("wot", [D, D], F32, kind="ExternalInput")
    t.wga_d = nc.dram_tensor("wgate", [D, I], F32, kind="ExternalInput")
    t.wup_d = nc.dram_tensor("wup", [D, I], F32, kind="ExternalInput")
    t.wdn_d = nc.dram_tensor("wdown", [I, D], F32, kind="ExternalInput")
    for nm, shape in (("c_ident", [128, 128]), ("c_dts", [128, 128]),
                      ("c_iotap1", [128, 128]), ("c_maskut", [128, 128]),
                      ("c_oh127", [128, 1]), ("c_ones", [1, 128]),
                      ("c_decay_beta", [1, NH]), ("c_lngamma", [1, D]),
                      ("c_lnbeta", [1, D])):
        setattr(t, nm, nc.dram_tensor(nm, shape, F32, kind="ExternalInput"))
    for nm, srcd in (("wqt", t.wqt_d), ("wvt", t.wvt_d), ("wgt", t.wgt_d),
                     ("wot", t.wot_d), ("wga", t.wga_d), ("wup", t.wup_d),
                     ("wdn", t.wdn_d)):
        r, c = srcd.shape
        setattr(t, nm + "_s", nc.dram_tensor("s_%s" % nm, [r, 2 * c], BF16))
    t.retc_d = nc.dram_tensor("s_retc", [128, D], F32)

    cm = contextlib.ExitStack()

    def sb(name, shape, dt=F32):
        return cm.enter_context(nc.sbuf_tensor(name, shape, dt))

    with cm:
        t.x_sb = sb("x_sb", [128, CH * D])
        t.va = sb("va", [128, CH * D])
        t.lnH = sb("lnH", [128, CH * D], BF16)
        t.lnL = sb("lnL", [128, CH * D], BF16)
        t.hg = sb("hg", [128, 2 * D])
        t.hsH = sb("hsH", [128, 2 * D], BF16)
        t.hsL = sb("hsL", [128, 2 * D], BF16)
        t.wa = sb("wa", [128, 2 * 2048], BF16)
        t.wb = sb("wb", [128, 2 * 2048], BF16)
        t.wd = sb("wd", [128, 2 * 4096], BF16)
        t.scr = sb("scr", [128, D])
        t.scg = sb("scg", [128, 512])
        t.wt = sb("wt", [128, 256])
        t.a_t = sb("a_t", [128, NH * 128])
        t.sp = sb("sp", [128, NH * 128])
        t.gbc = sb("gbc", [128, D])
        t.bbc = sb("bbc", [128, D])
        t.ident = sb("ident", [128, 128])
        t.oh127 = sb("oh127", [128, 1])
        t.ones1 = sb("ones1", [1, 128])
        t.lamv = sb("lamv", [1, NH])
        t.lnl = sb("lnl", [1, NH])
        t.lnl_bc = sb("lnl_bc", [128, NH])
        t.st6 = sb("st6", [128, 24])
        t.ag2 = sb("ag2", [128, 2])
        t.rstd = sb("rstd", [128, 1])
        t.epsv = sb("epsv", [128, 1])
        t.pb = [cm.enter_context(nc.psum_tensor("pb%d" % i, [128, 512], F32))
                for i in range(8)]

        sems = {s: cm.enter_context(nc.semaphore(s))
                for s in list(ENG_SEM.values()) + DMA_SEMS}

        p = Prog()
        t.par = 0
        t.weave = []
        t.dpar = 0
        emit_setup(p, t)
        for c in range(CH):
            p.op("sync", lambda E, c=c: E.dma_start(
                t.x_sb[:, c * D:(c + 1) * D], t.x_d[c * 128:(c + 1) * 128, :]),
                 writes=["x:%d" % c], sem="qx%d" % c)
        emit_weight_split(p, t)
        for k in range(nt):
            emit_tile(p, t, k)
        for c in range(4):
            p.stream["gpsimd"].append(("wait", "qo%d" % c, p.cnt["qo%d" % c]))
        p.materialize(nc, sems)
    return nc



def emit_setup(p, t):
    dts = t.hg[:, 0:128]
    iop1 = t.hg[:, 128:256]
    maskut = t.hg[:, 256:384]
    loads = [(t.ident[:, :], t.c_ident, "ident"), (dts, t.c_dts, "hg:0"),
             (iop1, t.c_iotap1, "hg:1"), (maskut, t.c_maskut, "hg:2"),
             (t.oh127[:, :], t.c_oh127, "oh127"), (t.ones1[:, :], t.c_ones, "ones1"),
             (t.lamv[:, :], t.c_decay_beta, "lamv")]
    for s, d, key in loads:
        p.op("sync", (lambda E, s=s, d=d: E.dma_start(s, d[:, :])), writes=[key],
             sem="qc")
    p.op("sync", lambda E: E.dma_start(t.scr[0:1, :], t.c_lngamma[:, :]), writes=["scr"],
         sem="qc")

    p.op("vector", lambda E: E.memset(t.epsv[:, :], LN_EPS), writes=["epsv"])
    p.op("scalar", lambda E: E.activation(t.lamv[:, :], t.lamv[:, :], AF.Exp, scale=-LN2),
         reads=["lamv"], writes=["lamv"])
    p.op("vector", lambda E: E.tensor_scalar(t.lamv[:, :], t.lamv[:, :], 1.0, -1.0,
                                             op0=ALU.subtract, op1=ALU.mult),
         reads=["lamv"], writes=["lamv"])
    p.op("vector", lambda E: E.tensor_scalar(t.lamv[:, :], t.lamv[:, :], TINY, 1.0 - 1e-9,
                                             op0=ALU.max, op1=ALU.min),
         reads=["lamv"], writes=["lamv"])
    p.op("scalar", lambda E: E.activation(t.lnl[:, :], t.lamv[:, :], AF.Ln),
         reads=["lamv"], writes=["lnl"])

    p.op("tensor", lambda E: E.matmul(t.pb[0][:, 0:NH], t.ones1[0:1, :], t.lnl[0:1, :],
                                      start=True, stop=True),
         reads=["lnl", "ones1"], writes=["pb0"])
    p.op("vector", lambda E: E.tensor_copy(t.lnl_bc[:, :], t.pb[0][:, 0:NH]),
         reads=["pb0"], writes=["lnl_bc"])
    for j in range(4):
        s0, s1 = j * 512, (j + 1) * 512
        p.op("tensor", lambda E, s0=s0, s1=s1: E.matmul(
            t.pb[1][:, :], t.ones1[0:1, :], t.scr[0:1, s0:s1], start=True, stop=True),
             reads=["scr", "ones1"], writes=["pb1"])
        p.op("vector", lambda E, s0=s0, s1=s1: E.tensor_copy(t.gbc[:, s0:s1], t.pb[1][:, :]),
             reads=["pb1"], writes=["gbc"])
    p.op("sync", lambda E: E.dma_start(t.scr[0:1, :], t.c_lnbeta[:, :]), writes=["scr"],
         sem="qc")
    for j in range(4):
        s0, s1 = j * 512, (j + 1) * 512
        p.op("tensor", lambda E, s0=s0, s1=s1: E.matmul(
            t.pb[2][:, :], t.ones1[0:1, :], t.scr[0:1, s0:s1], start=True, stop=True),
             reads=["scr", "ones1"], writes=["pb2"])
        p.op("vector", lambda E, s0=s0, s1=s1: E.tensor_copy(t.bbc[:, s0:s1], t.pb[2][:, :]),
             reads=["pb2"], writes=["bbc"])

    a3 = t.a_t[:, :].rearrange("p (h u) -> p h u", h=NH)
    s3 = t.sp[:, :].rearrange("p (h u) -> p h u", h=NH)
    lnl3 = _bc(t.lnl_bc[:, :], 2, 128)
    p.op("vector", lambda E, dts=dts: E.tensor_tensor(a3, _bc(dts, 1, NH), lnl3, op=ALU.mult),
         reads=["hg:0", "lnl_bc"], writes=["a_t"])
    p.op("scalar", lambda E: E.activation(t.a_t[:, :], t.a_t[:, :], AF.Exp),
         reads=["a_t"], writes=["a_t"])
    p.op("vector", lambda E, maskut=maskut: E.tensor_tensor(a3, a3, _bc(maskut, 1, NH), op=ALU.mult),
         reads=["a_t", "hg:2"], writes=["a_t"])
    p.op("vector", lambda E, iop1=iop1: E.tensor_tensor(s3, _bc(iop1, 1, NH), lnl3, op=ALU.mult),
         reads=["hg:1", "lnl_bc"], writes=["sp"])
    p.op("scalar", lambda E: E.activation(t.sp[:, :], t.sp[:, :], AF.Exp),
         reads=["sp"], writes=["sp"])
    oh3 = bass.AP(t.oh127, 0, [list(t.oh127[:, :].ap[0]), [0, NH], [0, 128]])
    p.op("vector", lambda E: E.tensor_tensor(s3, s3, oh3, op=ALU.mult),
         reads=["sp", "oh127"], writes=["sp"])


def emit_weight_split(p, t):
    ji = 0
    for src_d, dst in ((t.wqt_d, t.wqt_s), (t.wvt_d, t.wvt_s),
                       (t.wgt_d, t.wgt_s), (t.wot_d, t.wot_s),
                       (t.wga_d, t.wga_s), (t.wup_d, t.wup_s),
                       (t.wdn_d, t.wdn_s)):
        rows, cols = src_d.shape
        for kr in range(rows // 128):
            for kc in range(cols // 2048):
                sl = ji % 4
                ji += 1
                stg = t.va[:, sl * 2048:(sl + 1) * 2048]
                host = t.lnH if sl < 2 else t.lnL
                base = (sl % 2) * 4096
                pair = host[:, base:base + 4096]
                hiv = bass.AP(host, pair.offset, [list(pair.ap[0]), [1024, 4], [1, 512]])
                lov = bass.AP(host, pair.offset + 512, [list(pair.ap[0]), [1024, 4], [1, 512]])
                sgv = bass.AP(t.va, stg.offset, [list(stg.ap[0]), [512, 4], [1, 512]])
                kin = ["va:%d" % sl]
                kh = "lnH" if sl < 2 else "lnL"
                kpair = ["%s:%d" % (kh, base + j * 128) for j in range(32)]
                sap = src_d[kr * 128:(kr + 1) * 128, kc * 2048:(kc + 1) * 2048]
                p.op("sync", lambda E, stg=stg, sap=sap: E.dma_start(stg, sap),
                     writes=kin, sem="qsi%d" % sl)
                p.op("scalar", lambda E, hiv=hiv, sgv=sgv: E.copy(hiv, sgv),
                     reads=kin, writes=kpair[:16])
                p.op("vector", lambda E, lov=lov, sgv=sgv, hiv=hiv: E.tensor_tensor(
                    lov, sgv, hiv, op=ALU.subtract),
                     reads=kin + kpair[:1], writes=kpair[16:])
                off = kr * 128 * 2 * cols + kc * 4096
                dap = bass.AP(dst, off, [[2 * cols, 128], [1, 4096]])
                oeng = "scalar" if sl < 2 else "gpsimd"
                p.op(oeng, lambda E, dap=dap, pair=pair: E.dma_start(dap, pair),
                     reads=kpair, writes=["sW:%s" % dst.name], sem="qsp%d" % sl)



def _layernorm(p, t, src, src_keys, dst, dst_keys, affine=False):
    for j in range(4):
        sj = bass.AP(src.tensor, src.offset + j * 512, [list(src.ap[0]), [1, 512]])
        p.op("vector", lambda E, sj=sj, j=j: E.bn_stats(t.st6[:, j * 6:(j + 1) * 6], sj),
             reads=src_keys, writes=["st6:%d" % j])
    p.op("vector", lambda E: E.bn_aggr(t.ag2[:, :], t.st6[:, :]),
         reads=["st6:%d" % j for j in range(4)], writes=["ag2"])
    p.op("scalar", lambda E: E.activation(t.rstd[:, :], t.ag2[:, 1:2], AF.Sqrt,
                                          bias=t.epsv[:, 0:1]),
         reads=["ag2", "epsv"], writes=["rstd"])
    p.op("vector", lambda E: E.reciprocal(t.rstd[:, :], t.rstd[:, :]),
         reads=["rstd"], writes=["rstd"])
    p.op("vector", lambda E: E.tensor_scalar(dst, src, t.ag2[:, 0:1], t.rstd[:, 0:1],
                                             op0=ALU.subtract, op1=ALU.mult),
         reads=list(src_keys) + ["ag2", "rstd"], writes=list(dst_keys))
    if affine:
        p.op("vector", lambda E: E.tensor_tensor(dst, dst, t.gbc[:, :], op=ALU.mult),
             reads=list(dst_keys) + ["gbc"], writes=list(dst_keys))
        p.op("vector", lambda E: E.tensor_tensor(dst, dst, t.bbc[:, :], op=ALU.add),
             reads=list(dst_keys) + ["bbc"], writes=list(dst_keys))


def _transpose_grp(p, t, src, src_keys, c, grp):
    bank = 4 + (grp % 2)
    fns = []
    for j in range(4):
        dc = grp * 4 + j
        sblk = bass.AP(src.tensor, src.offset + dc * 128,
                       [list(src.ap[0]), [1, 128]])
        fns.append(lambda E, sblk=sblk, j=j, bank=bank: E.transpose(
            t.pb[bank][:, j * 128:(j + 1) * 128], sblk, t.ident[:, :]))
    p.op("tensor", fns, reads=list(src_keys) + ["ident"], writes=["pb%d" % bank])
    cols = [((grp * 4 + j) * CH + c) * 128 for j in range(4)]
    hk = ["lnH:%d" % co for co in cols]
    lk = ["lnL:%d" % co for co in cols]

    def evh(E, bank=bank, cols=cols):
        inst = None
        for j, co in enumerate(cols):
            inst = E.copy(t.lnH[:, co:co + 128],
                          t.pb[bank][:, j * 128:(j + 1) * 128])
        return inst

    def evl(E, bank=bank, cols=cols):
        inst = None
        for j, co in enumerate(cols):
            inst = E.tensor_tensor(t.lnL[:, co:co + 128],
                                   t.pb[bank][:, j * 128:(j + 1) * 128],
                                   t.lnH[:, co:co + 128], op=ALU.subtract)
        return inst
    p.op("scalar", evh, reads=["pb%d" % bank], writes=hk)
    p.op("vector", evl, reads=["pb%d" % bank] + hk, writes=lk)


def _transpose_band(p, t, src, src_keys, c):
    for grp in range(4):
        _transpose_grp(p, t, src, src_keys, c, grp)


def _proj_closures(p, t, wsd, colbase, itile, evac, bank_base=0):
    cls = []
    for dq in range(8):
        def go(dq=dq):
            pq = t.par % 2
            t.par += 1
            grp = (colbase + itile * 512) // 512
            dap = bass.AP(wsd, dq * 256 * 4096 + grp * 1024,
                          [[4096, 128], [128 * 4096, 2], [1, 1024]])
            sap = t.wa[:, pq * 2048:(pq + 1) * 2048].rearrange(
                "p (c j) -> p c j", c=2)
            p.op("sync", lambda E, dap=dap, sap=sap: E.dma_start(sap, dap),
                 reads=["sW:%s" % wsd.name],
                 writes=["wa:%d" % pq], sem="qwa%d" % pq)
            fns = []
            rk = ["wa:%d" % pq]
            for dl in range(2):
                dc = dq * 2 + dl
                for tt in range(CH):
                    co = (dc * CH + tt) * 128
                    lhsH = t.lnH[:, co:co + 128]
                    rhsH = t.wa[:, pq * 2048 + dl * 1024: pq * 2048 + dl * 1024 + 512]
                    rhsL = t.wa[:, pq * 2048 + dl * 1024 + 512: pq * 2048 + dl * 1024 + 1024]
                    st = (dc == 0)
                    fns.append(lambda E, lhsH=lhsH, rhsH=rhsH, tt=tt, st=st: E.matmul(
                        t.pb[bank_base + tt][:, :], lhsH, rhsH, start=st, stop=False))
                    fns.append(lambda E, lhsH=lhsH, rhsL=rhsL, tt=tt: E.matmul(
                        t.pb[bank_base + tt][:, :], lhsH, rhsL, start=False, stop=False))
                    rk += ["lnH:%d" % co, "lnL:%d" % co]
            for dl in range(2):
                dc = dq * 2 + dl
                for tt in range(CH):
                    co = (dc * CH + tt) * 128
                    lhsL = t.lnL[:, co:co + 128]
                    rhsH = t.wa[:, pq * 2048 + dl * 1024: pq * 2048 + dl * 1024 + 512]
                    sp_ = (dc == 15)
                    fns.append(lambda E, lhsL=lhsL, rhsH=rhsH, tt=tt, sp_=sp_: E.matmul(
                        t.pb[bank_base + tt][:, :], lhsL, rhsH, start=False, stop=sp_))
            p.op("tensor", fns, reads=rk,
                 writes=["pb%d" % (bank_base + tt) for tt in range(CH)])
        cls.append(go)

    def evacs():
        for tt in range(CH):
            evac(tt, itile, bank_base)
        for _ in range(12):
            if t.weave:
                t.weave.pop(0)()
    cls.append(evacs)
    return cls


def _proj_unit(p, t, wsd, colbase, itile, evac, bank_base=0):
    for c in _proj_closures(p, t, wsd, colbase, itile, evac, bank_base):
        c()



def emit_tile(p, t, k):
    tok0 = k * TILE
    seq_tile = k % TPS

    def xk(c):
        return "x:%d" % c

    def vak(c):
        return "va:%d" % c

    xband = lambda c: t.x_sb[:, c * D:(c + 1) * D]
    vband = lambda c: t.va[:, c * D:(c + 1) * D]

    if seq_tile != 0:
        p.op("gpsimd", lambda E: E.dma_start(t.scr[:, :], t.retc_d[:, :]),
             writes=["scr"], sem="qrc")
    if k != 0:
        for c in range(CH):
            p.op("sync", lambda E, c=c: E.dma_start(
                xband(c), t.x_d[tok0 + c * 128: tok0 + (c + 1) * 128, :]), writes=[xk(c)],
                 sem="qx%d" % c)
    for c in range(CH):
        _layernorm(p, t, xband(c), [xk(c)], vband(c), [vak(c)])
    def evac_v(tt, itile, bb=0):
        sl = t.va[:, tt * D + itile * 512: tt * D + (itile + 1) * 512]
        p.op("scalar", lambda E, sl=sl, b=bb + tt: E.copy(sl, t.pb[b][:, :]),
             reads=["pb%d" % (bb + tt)], writes=[vak(tt)])

    v0 = _proj_closures(p, t, t.wvt_s, 0, 0, evac_v)
    for c in range(CH):
        _transpose_grp(p, t, vband(c), [vak(c)], c, 0)
    for dq in range(8):
        v0[dq]()
        if dq % 2 == 1 and dq < 7:
            for c in range(CH):
                _transpose_grp(p, t, vband(c), [vak(c)], c, (dq + 1) // 2)
    v0[8]()
    for itile in range(1, 4):
        _proj_unit(p, t, t.wvt_s, 0, itile, evac_v, bank_base=4 * (itile % 2))

    def ret_chunk(c):
        seq_start = (seq_tile == 0 and c == 0)
        fns = []
        reads = [vak(c), "a_t", "sp"]
        for b in range(4):
            for hh in range(4):
                h = 4 * b + hh
                vsl = t.va[:, c * D + h * 128: c * D + (h + 1) * 128]
                psl = t.pb[4 + b][:, hh * 128:(hh + 1) * 128]
                asl = t.a_t[:, h * 128:(h + 1) * 128]
                fns.append(lambda E, psl=psl, asl=asl, vsl=vsl, hh=hh, ss=seq_start:
                           E.matmul(psl, asl, vsl, start=(hh == 0),
                                    stop=(hh == 3 and ss)))
                if not seq_start:
                    prev = (t.va[:, (c - 1) * D + h * 128: (c - 1) * D + (h + 1) * 128]
                            if c > 0 else t.scr[:, h * 128:(h + 1) * 128])
                    ssl = t.sp[:, h * 128:(h + 1) * 128]
                    fns.append(lambda E, psl=psl, ssl=ssl, prev=prev, hh=hh:
                               E.matmul(psl, ssl, prev, start=False, stop=(hh == 3)))
        if not seq_start:
            reads.append(vak(c - 1) if c > 0 else "scr")
        p.op("tensor", fns, reads=reads, writes=["pb%d" % (4 + b) for b in range(4)])
        for b in range(4):
            sl = t.va[:, c * D + b * 512: c * D + (b + 1) * 512]
            p.op("scalar", lambda E, sl=sl, b=b: E.copy(sl, t.pb[4 + b][:, :]),
                 reads=["pb%d" % (4 + b)], writes=[vak(c)])
    def ret_carry():
        if seq_tile != TPS - 1:
            p.op("gpsimd", lambda E: E.dma_start(t.retc_d[:, :], vband(3)),
                 reads=[vak(3)], writes=["retcd"], sem="qrc")

    def evac_q(tt, itile, bb=0):
        sl = t.va[:, tt * D + itile * 512: tt * D + (itile + 1) * 512]
        p.op("vector", lambda E, sl=sl, b=bb + tt: E.tensor_tensor(
            sl, t.pb[b][:, :], sl, op=ALU.mult),
             reads=["pb%d" % (bb + tt), vak(tt)], writes=[vak(tt)])

    q0 = _proj_closures(p, t, t.wqt_s, 0, 0, evac_q)
    ret_chunk(0)
    q0[0]()
    q0[1]()
    ret_chunk(1)
    q0[2]()
    q0[3]()
    ret_chunk(2)
    q0[4]()
    q0[5]()
    ret_chunk(3)
    ret_carry()
    q0[6]()
    q0[7]()
    q0[8]()
    for itile in range(1, 4):
        _proj_unit(p, t, t.wqt_s, 0, itile, evac_q, bank_base=4 * (itile % 2))

    for c in range(CH):
        _layernorm(p, t, vband(c), [vak(c)], vband(c), [vak(c)], affine=True)

    def evac_g(tt, itile, bb=0):
        gsl = t.scr[:, tt * 512:(tt + 1) * 512]
        osl = t.va[:, tt * D + itile * 512: tt * D + (itile + 1) * 512]
        p.op("scalar", lambda E, gsl=gsl, b=bb + tt: E.activation(gsl, t.pb[b][:, :], AF.Silu),
             reads=["pb%d" % (bb + tt)], writes=["scr"]),
        p.op("vector", lambda E, osl=osl, gsl=gsl: E.tensor_tensor(
            osl, osl, gsl, op=ALU.mult),
             reads=[vak(tt), "scr"], writes=[vak(tt)])

    for itile in range(4):
        _proj_unit(p, t, t.wgt_s, 0, itile, evac_g, bank_base=4 * (itile % 2))

    def evac_o(tt, itile, bb=0):
        xsl = t.x_sb[:, tt * D + itile * 512: tt * D + (itile + 1) * 512]
        p.op("vector", lambda E, xsl=xsl, b=bb + tt: E.tensor_tensor(
            xsl, t.pb[b][:, :], xsl, op=ALU.add),
             reads=["pb%d" % (bb + tt), xk(tt)], writes=[xk(tt)])

    o0 = _proj_closures(p, t, t.wot_s, 0, 0, evac_o)
    for c in range(CH):
        _transpose_grp(p, t, vband(c), [vak(c)], c, 0)
    for dq in range(8):
        o0[dq]()
        if dq % 2 == 1 and dq < 7:
            for c in range(CH):
                _transpose_grp(p, t, vband(c), [vak(c)], c, (dq + 1) // 2)
    o0[8]()
    for itile in range(1, 4):
        _proj_unit(p, t, t.wot_s, 0, itile, evac_o, bank_base=4 * (itile % 2))

    for c in range(CH):
        _layernorm(p, t, xband(c), [xk(c)], vband(c), [vak(c)])
    for c in range(CH):
        _transpose_band(p, t, vband(c), [vak(c)], c)

    for tp in range(2):
        ts = (2 * tp, 2 * tp + 1)

        def gu(s, half, ts=ts):
            ig, ist = divmod(s, 4)
            colbase = ig * 2048 + ist * 512
            for dq in range(4 * half, 4 * half + 4):
                pq = t.par % 2
                t.par += 1
                for wsd, wbuf, wk, qs_, bb in (
                        (t.wga_s, t.wa, "wa", "qwa", 0),
                        (t.wup_s, t.wb, "wb", "qwb", 2)):
                    dap = bass.AP(wsd, dq * 256 * 16384 + (colbase // 512) * 1024,
                                  [[16384, 128], [128 * 16384, 2], [1, 1024]])
                    sap = wbuf[:, pq * 2048:(pq + 1) * 2048].rearrange(
                        "p (c j) -> p c j", c=2)
                    p.op("sync", lambda E, dap=dap, sap=sap: E.dma_start(sap, dap),
                         reads=["sW:%s" % wsd.name],
                         writes=["%s:%d" % (wk, pq)], sem="%s%d" % (qs_, pq))
                    fns = []
                    rk = ["%s:%d" % (wk, pq)]
                    for dl in range(2):
                        dc = dq * 2 + dl
                        for tl in range(2):
                            tt_ = ts[tl]
                            co = (dc * CH + tt_) * 128
                            lhsH = t.lnH[:, co:co + 128]
                            rhsH = wbuf[:, pq * 2048 + dl * 1024: pq * 2048 + dl * 1024 + 512]
                            rhsL = wbuf[:, pq * 2048 + dl * 1024 + 512: pq * 2048 + dl * 1024 + 1024]
                            st = (dc == 0)
                            fns.append(lambda E, lhsH=lhsH, rhsH=rhsH, tl=tl, st=st, bb=bb:
                                       E.matmul(t.pb[bb + tl][:, :], lhsH, rhsH,
                                                start=st, stop=False))
                            fns.append(lambda E, lhsH=lhsH, rhsL=rhsL, tl=tl, bb=bb:
                                       E.matmul(t.pb[bb + tl][:, :], lhsH, rhsL,
                                                start=False, stop=False))
                            rk += ["lnH:%d" % co, "lnL:%d" % co]
                    for dl in range(2):
                        dc = dq * 2 + dl
                        for tl in range(2):
                            tt_ = ts[tl]
                            co = (dc * CH + tt_) * 128
                            lhsL = t.lnL[:, co:co + 128]
                            rhsH = wbuf[:, pq * 2048 + dl * 1024: pq * 2048 + dl * 1024 + 512]
                            sp_ = (dc == 15)
                            fns.append(lambda E, lhsL=lhsL, rhsH=rhsH, tl=tl, sp_=sp_, bb=bb:
                                       E.matmul(t.pb[bb + tl][:, :], lhsL, rhsH,
                                                start=False, stop=sp_))
                    p.op("tensor", fns, reads=rk,
                         writes=["pb%d" % (bb + tl) for tl in range(2)])

        def eps(s):
            ig, ist = divmod(s, 4)
            for tl in range(2):
                hsl = t.hg[:, tl * D + ist * 512: tl * D + (ist + 1) * 512]
                hkeys = ["hg:%d" % (tl * DC + ist * 4 + j) for j in range(4)]
                p.op("scalar", lambda E, hsl=hsl, tl=tl: E.activation(
                    hsl, t.pb[tl][:, :], AF.Silu),
                     reads=["pb%d" % tl], writes=hkeys)
                p.op("vector", lambda E, hsl=hsl, tl=tl: E.tensor_tensor(
                    hsl, hsl, t.pb[2 + tl][:, :], op=ALU.mult),
                     reads=["pb%d" % (2 + tl)] + hkeys, writes=hkeys)

        def spars(s):
            ig, ist = divmod(s, 4)
            for tl in range(2):
                _sparsify_sub(p, t, tl, ist)

        def trans(s):
            ig, ist = divmod(s, 4)
            for tl in range(2):
                bank = 4 + tl
                fns = []
                for j in range(4):
                    ic = ist * 4 + j
                    hsl = t.hg[:, tl * D + ic * 128: tl * D + (ic + 1) * 128]
                    fns.append(lambda E, hsl=hsl, j=j, bank=bank: E.transpose(
                        t.pb[bank][:, j * 128:(j + 1) * 128], hsl, t.ident[:, :]))
                bkeys = ["hg:%d" % (tl * DC + ist * 4 + j) for j in range(4)]
                p.op("tensor", fns, reads=bkeys + ["ident"], writes=["pb%d" % bank])
                hco = tl * D + ist * 512
                hk = ["hsH:%d" % (tl * DC + ist * 4 + j) for j in range(4)]
                lk = ["hsL:%d" % (tl * DC + ist * 4 + j) for j in range(4)]
                p.op("scalar", lambda E, bank=bank, hco=hco: E.copy(
                    t.hsH[:, hco:hco + 512], t.pb[bank][:, :]),
                     reads=["pb%d" % bank], writes=hk)
                p.op("vector", lambda E, bank=bank, hco=hco: E.tensor_tensor(
                    t.hsL[:, hco:hco + 512], t.pb[bank][:, :],
                    t.hsH[:, hco:hco + 512], op=ALU.subtract),
                     reads=["pb%d" % bank] + hk, writes=lk)

        def down(s, nh, ts=ts):
            ig, ist = divmod(s, 4)
            if True:
                for icp in range(2):
                    pq = t.dpar % 2
                    t.dpar += 1
                    r0 = ig * 2048 + (ist * 4 + icp * 2) * 128
                    dap = bass.AP(t.wdn_s, r0 * 4096 + nh * 2048,
                                  [[4096, 128], [128 * 4096, 2], [1, 2048]])
                    sap = t.wd[:, pq * 4096:(pq + 1) * 4096].rearrange(
                        "p (c j) -> p c j", c=2)
                    p.op("gpsimd", lambda E, dap=dap, sap=sap: E.dma_start(sap, dap),
                         reads=["sW:%s" % t.wdn_s.name],
                         writes=["wd:%d" % pq], sem="qwd%d" % pq)
                    fns = []
                    rks = ["wd:%d" % pq]
                    for icl2 in range(2):
                        icl = icp * 2 + icl2
                        ic = ist * 4 + icl
                        rks += ["hsH:%d" % ic, "hsL:%d" % ic,
                                "hsH:%d" % (DC + ic), "hsL:%d" % (DC + ic)]
                        for tl in range(2):
                            hco = tl * D + ic * 128
                            lhsH = t.hsH[:, hco:hco + 128]
                            lhsL = t.hsL[:, hco:hco + 128]
                            for nl in range(2):
                                bank = 4 + tl * 2 + nl
                                rhsH = t.wd[:, pq * 4096 + icl2 * 2048 + nl * 1024:
                                            pq * 4096 + icl2 * 2048 + nl * 1024 + 512]
                                rhsL = t.wd[:, pq * 4096 + icl2 * 2048 + nl * 1024 + 512:
                                            pq * 4096 + icl2 * 2048 + nl * 1024 + 1024]
                                fns.append(lambda E, lhsH=lhsH, rhsH=rhsH, bank=bank, icl=icl:
                                           E.matmul(t.pb[bank][:, :], lhsH, rhsH,
                                                    start=(icl == 0), stop=False))
                                fns.append(lambda E, lhsH=lhsH, rhsL=rhsL, bank=bank:
                                           E.matmul(t.pb[bank][:, :], lhsH, rhsL,
                                                    start=False, stop=False))
                            for nl in range(2):
                                bank = 4 + tl * 2 + nl
                                rhsH = t.wd[:, pq * 4096 + icl2 * 2048 + nl * 1024:
                                            pq * 4096 + icl2 * 2048 + nl * 1024 + 512]
                                fns.append(lambda E, lhsL=lhsL, rhsH=rhsH, bank=bank, icl=icl:
                                           E.matmul(t.pb[bank][:, :], lhsL, rhsH,
                                                    start=False, stop=(icl == 3)))
                    p.op("tensor", fns, reads=rks,
                         writes=["pb%d" % (4 + j) for j in range(4)])
                for tl in range(2):
                    tt_ = ts[tl]
                    for nl in range(2):
                        n = nh * 2 + nl
                        bank = 4 + tl * 2 + nl
                        ysl = t.va[:, tt_ * D + n * 512: tt_ * D + (n + 1) * 512]
                        if s == 0:
                            p.op("scalar", lambda E, ysl=ysl, bank=bank: E.copy(
                                ysl, t.pb[bank][:, :]),
                                 reads=["pb%d" % bank], writes=[vak(tt_)])
                        else:
                            p.op("vector", lambda E, ysl=ysl, bank=bank: E.tensor_tensor(
                                ysl, ysl, t.pb[bank][:, :], op=ALU.add),
                                 reads=["pb%d" % bank, vak(tt_)], writes=[vak(tt_)])

        for s in range(16):
            if s == 0:
                gu(0, 0)
                gu(0, 1)
                eps(0)
                spars(0)
            if s + 1 < 16:
                gu(s + 1, 0)
            trans(s)
            if s + 1 < 16:
                gu(s + 1, 1)
            down(s, 0)
            down(s, 1)
            if s + 1 < 16:
                eps(s + 1)
                spars(s + 1)

    for c in range(CH):
        p.op("vector", lambda E, c=c: E.tensor_tensor(
            vband(c), vband(c), xband(c), op=ALU.add),
             reads=[vak(c), xk(c)], writes=[vak(c)])
        p.op("gpsimd", lambda E, c=c: E.dma_start(
            t.out_d[tok0 + c * 128: tok0 + (c + 1) * 128, :], vband(c)),
             reads=[vak(c)], writes=["out:%d:%d" % (k, c)], sem="qo%d" % c)


def _sparsify_sub(p, t, tl, ist):
    col0 = tl * D + ist * 512
    sub = t.hg[:, col0:col0 + 512]
    bkeys = ["hg:%d" % (tl * DC + ist * 4 + j) for j in range(4)]
    p.op("scalar", lambda E: E.activation(t.scr[:, 0:512], sub, AF.Abs),
         reads=bkeys, writes=["scr"])
    a = [_stride4(t.scr, 0, i, 128) for i in range(4)]
    w0, w1 = t.wt[:, 0:128], t.wt[:, 128:256]
    acc = [t.scg[:, i * 128:(i + 1) * 128] for i in range(4)]
    ak = ["scg:%d" % i for i in range(4)]

    def tt(dst, x, y, op, dk, rk):
        p.op("vector", lambda E, dst=dst, x=x, y=y, op=op: E.tensor_tensor(
            dst, x, y, op=op), reads=rk, writes=dk)

    tt(w0, a[0], a[1], ALU.is_ge, ["wt0"], ["scr"])
    p.op("vector", lambda E: E.tensor_copy(acc[0], w0), reads=["wt0"], writes=[ak[0]])
    p.op("vector", lambda E: E.tensor_scalar(acc[1], w0, -1.0, None, op0=ALU.mult),
         reads=["wt0"], writes=[ak[1]])
    tt(w1, a[0], a[2], ALU.is_ge, ["wt1"], ["scr"])
    tt(acc[0], acc[0], w1, ALU.add, [ak[0]], [ak[0], "wt1"])
    p.op("vector", lambda E: E.tensor_scalar(acc[2], w1, -1.0, None, op0=ALU.mult),
         reads=["wt1"], writes=[ak[2]])
    tt(w0, a[0], a[3], ALU.is_ge, ["wt0"], ["scr"])
    tt(acc[0], acc[0], w0, ALU.add, [ak[0]], [ak[0], "wt0"])
    p.op("vector", lambda E: E.tensor_copy(acc[3], w0), reads=["wt0"], writes=[ak[3]])
    tt(w1, a[1], a[2], ALU.is_ge, ["wt1"], ["scr"])
    tt(acc[1], acc[1], w1, ALU.add, [ak[1]], [ak[1], "wt1"])
    tt(acc[2], acc[2], w1, ALU.subtract, [ak[2]], [ak[2], "wt1"])
    tt(w0, a[1], a[3], ALU.is_ge, ["wt0"], ["scr"])
    tt(acc[1], acc[1], w0, ALU.add, [ak[1]], [ak[1], "wt0"])
    tt(acc[3], acc[3], w0, ALU.add, [ak[3]], [ak[3], "wt0"])
    tt(w1, a[2], a[3], ALU.is_ge, ["wt1"], ["scr"])
    tt(acc[2], acc[2], w1, ALU.add, [ak[2]], [ak[2], "wt1"])
    tt(acc[3], acc[3], w1, ALU.add, [ak[3]], [ak[3], "wt1"])
    specs = [(0, ALU.is_ge, 2.0), (1, ALU.is_ge, 1.0), (2, ALU.is_ge, 0.0),
             (3, ALU.is_le, 1.0)]
    for i, cmp, thr in specs:
        hi = _stride4(t.hg, col0, i, 128)
        p.op("vector", lambda E, hi=hi, ai=acc[i], cmp=cmp, thr=thr:
             E.scalar_tensor_tensor(hi, ai, thr, hi, op0=cmp, op1=ALU.mult),
             reads=[ak[i]] + bkeys, writes=bkeys)



def _host_inputs(x_shard, ins):
    m = {
        "x": np.ascontiguousarray(np.asarray(x_shard, np.float32).reshape(TOK, D)),
        "wqt": np.ascontiguousarray(np.asarray(ins["L0_attn_Wq"], np.float32).T),
        "wvt": np.ascontiguousarray(np.asarray(ins["L0_attn_Wv"], np.float32).T),
        "wgt": np.ascontiguousarray(np.asarray(ins["L0_attn_Wg"], np.float32).T),
        "wot": np.ascontiguousarray(np.asarray(ins["L0_attn_Wo"], np.float32).T),
        "wgate": np.ascontiguousarray(np.asarray(ins["L0_ffn_w_gate"], np.float32)),
        "wup": np.ascontiguousarray(np.asarray(ins["L0_ffn_w_up"], np.float32)),
        "wdown": np.ascontiguousarray(np.asarray(ins["L0_ffn_w_down"], np.float32)),
        "c_ident": np.eye(128, dtype=np.float32),
        "c_dts": (np.arange(128, dtype=np.float32)[None, :]
                  - np.arange(128, dtype=np.float32)[:, None]),
        "c_iotap1": np.broadcast_to(np.arange(1, 129, dtype=np.float32)[None, :],
                                    (128, 128)).copy(),
        "c_maskut": np.triu(np.ones((128, 128), dtype=np.float32)),
        "c_oh127": (np.arange(128) == 127).astype(np.float32)[:, None].copy(),
        "c_ones": np.ones((1, 128), dtype=np.float32),
        "c_decay_beta": np.asarray(ins["L0_attn_beta"], np.float32).reshape(1, NH),
        "c_lngamma": np.asarray(ins["L0_attn_ln_gamma"], np.float32).reshape(1, D),
        "c_lnbeta": np.asarray(ins["L0_attn_ln_beta"], np.float32).reshape(1, D),
    }
    return m


def kernel(**inputs):
    x = np.asarray(inputs["L0_attn_x"], np.float32)
    nc = build_nc()
    per = B // NCORES
    shared = _host_inputs(x[0:per], inputs)
    in_maps = []
    for i in range(NCORES):
        m = dict(shared)
        m["x"] = np.ascontiguousarray(x[i * per:(i + 1) * per].reshape(TOK, D))
        in_maps.append(m)
    res = run_bass_kernel_spmd(nc, in_maps, core_ids=list(range(NCORES)))
    shards = [np.asarray(r["out"], np.float32).reshape(per, L, D) for r in res.results]
    return np.concatenate(shards, axis=0)


# version 18
# speedup vs baseline: 1.2976x; 1.0519x over previous
import contextlib
import numpy as np
import concourse.bass as bass
import concourse.mybir as mybir
from concourse.bass_utils import run_bass_kernel_spmd

F32 = mybir.dt.float32
BF16 = mybir.dt.bfloat16
AF = mybir.ActivationFunctionType
ALU = mybir.AluOpType

B, L, D, I, NH = 16, 2048, 2048, 8192, 16
NCORES = 8
TOK = (B // NCORES) * L
TILE = 512
CH = TILE // 128
DC = D // 128
NT = TOK // TILE
TPS = L // TILE
LN_EPS = 1e-5
LN2 = float(np.log(2.0))
TINY = float(np.finfo(np.float32).tiny)

ENG_SEM = {"tensor": "te", "vector": "ve", "scalar": "se"}
DMA_SEMS = (["qc", "qwa0", "qwa1", "qwa2", "qwb0", "qwb1", "qwb2", "qwd0", "qwd1", "qrc"]
            + ["qsi%d" % c for c in range(4)] + ["qsp%d" % c for c in range(4)]
            + ["qx%d" % c for c in range(4)] + ["qo%d" % c for c in range(4)])
ALL_ENGS = ("tensor", "vector", "scalar", "sync", "gpsimd")


def _bc(ap, pos, n):
    dims = [list(d) for d in ap.ap]
    dims.insert(pos, [0, n])
    return bass.AP(ap.tensor, ap.offset, dims)


def _stride4(t, col0, i, cnt):
    base = t[:, :]
    return bass.AP(t, base.offset + col0 + i, [list(base.ap[0]), [4, cnt]])


def _flat(dram, k, width):
    return bass.AP(dram, k * 128 * width, [[width, 128], [1, width]])


class Prog:

    def __init__(self):
        self.stream = {e: [] for e in ALL_ENGS}
        self.cnt = {s: 0 for s in list(ENG_SEM.values()) + DMA_SEMS}
        self.last_w = {}
        self.readers = {}
        self.mark = {}

    def op(self, eng, fns, reads=(), writes=(), sem=None):
        if not isinstance(fns, (list, tuple)):
            fns = [fns]
        if sem is None:
            sem = ENG_SEM[eng]
        inc = 16 if sem in DMA_SEMS else 1
        deps = []
        for r in reads:
            lw = self.last_w.get(r)
            if lw:
                deps.append(lw)
        for w in writes:
            lw = self.last_w.get(w)
            if lw:
                deps.append(lw)
            deps.extend(self.readers.get(w, ()))
        for dsem, dval in deps:
            if dsem == sem and eng == "tensor":
                continue
            if dsem in DMA_SEMS:
                dval = self.cnt[dsem]
            key = (eng, dsem)
            if self.mark.get(key, 0) < dval:
                self.mark[key] = dval
                self.stream[eng].append(("wait", dsem, dval))
        self.stream[eng].append(("ops", fns, sem, inc))
        self.cnt[sem] += inc
        pt = (sem, self.cnt[sem])
        for w in writes:
            self.last_w[w] = pt
            self.readers[w] = []
        for r in reads:
            self.readers.setdefault(r, []).append(pt)

    def materialize(self, nc, sems):
        with nc.Block() as blk:
            for eng in ALL_ENGS:
                entries = self.stream[eng]

                def make(entries):
                    def body(E):
                        for ent in entries:
                            if ent[0] == "wait":
                                E.wait_ge(sems[ent[1]], ent[2])
                            else:
                                inst = None
                                for f in ent[1]:
                                    inst = f(E)
                                inst.then_inc(sems[ent[2]], ent[3])
                    return body

                getattr(blk, eng)(make(entries))


class NS:
    pass


def build_nc(nt=NT):
    nc = bass.Bass("TRN2", target_bir_lowering=False)
    t = NS()
    t.x_d = nc.dram_tensor("x", [TOK, D], F32, kind="ExternalInput")
    t.out_d = nc.dram_tensor("out", [TOK, D], F32, kind="ExternalOutput")
    t.wqt_d = nc.dram_tensor("wqt", [D, D], F32, kind="ExternalInput")
    t.wvt_d = nc.dram_tensor("wvt", [D, D], F32, kind="ExternalInput")
    t.wgt_d = nc.dram_tensor("wgt", [D, D], F32, kind="ExternalInput")
    t.wot_d = nc.dram_tensor("wot", [D, D], F32, kind="ExternalInput")
    t.wga_d = nc.dram_tensor("wgate", [D, I], F32, kind="ExternalInput")
    t.wup_d = nc.dram_tensor("wup", [D, I], F32, kind="ExternalInput")
    t.wdn_d = nc.dram_tensor("wdown", [I, D], F32, kind="ExternalInput")
    for nm, shape in (("c_ident", [128, 128]), ("c_dts", [128, 128]),
                      ("c_iotap1", [128, 128]), ("c_maskut", [128, 128]),
                      ("c_oh127", [128, 1]), ("c_ones", [1, 128]),
                      ("c_decay_beta", [1, NH]), ("c_lngamma", [1, D]),
                      ("c_lnbeta", [1, D])):
        setattr(t, nm, nc.dram_tensor(nm, shape, F32, kind="ExternalInput"))
    for nm, srcd in (("wqt", t.wqt_d), ("wvt", t.wvt_d), ("wgt", t.wgt_d),
                     ("wot", t.wot_d), ("wga", t.wga_d), ("wup", t.wup_d),
                     ("wdn", t.wdn_d)):
        r, c = srcd.shape
        setattr(t, nm + "_s", nc.dram_tensor("s_%s" % nm, [r, 2 * c], BF16))
    t.retc_d = nc.dram_tensor("s_retc", [128, D], F32)

    cm = contextlib.ExitStack()

    def sb(name, shape, dt=F32):
        return cm.enter_context(nc.sbuf_tensor(name, shape, dt))

    with cm:
        t.x_sb = sb("x_sb", [128, CH * D])
        t.va = sb("va", [128, CH * D])
        t.lnH = sb("lnH", [128, CH * D], BF16)
        t.lnL = sb("lnL", [128, CH * D], BF16)
        t.hg = sb("hg", [128, 2 * D])
        t.hsH = sb("hsH", [128, 2 * D], BF16)
        t.hsL = sb("hsL", [128, 2 * D], BF16)
        t.wa = sb("wa", [128, 3 * 2048], BF16)
        t.wb = sb("wb", [128, 2 * 2048], BF16)
        t.wd = sb("wd", [128, 2 * 4096], BF16)
        t.scr = sb("scr", [128, D])
        t.scg = sb("scg", [128, 512])
        t.wt = sb("wt", [128, 128])
        t.a_t = sb("a_t", [128, NH * 128])
        t.sp = sb("sp", [128, NH * 128])
        t.gbc = sb("gbc", [128, D])
        t.bbc = sb("bbc", [128, D])
        t.ident = sb("ident", [128, 128])
        t.oh127 = sb("oh127", [128, 1])
        t.ones1 = sb("ones1", [1, 128])
        t.lamv = sb("lamv", [1, NH])
        t.lnl = sb("lnl", [1, NH])
        t.lnl_bc = sb("lnl_bc", [128, NH])
        t.st6 = sb("st6", [128, 24])
        t.small = sb("small", [128, 8])
        t.ag2 = t.small
        t.rstd_ap = None
        t.pb = [cm.enter_context(nc.psum_tensor("pb%d" % i, [128, 512], F32))
                for i in range(8)]

        sems = {s: cm.enter_context(nc.semaphore(s))
                for s in list(ENG_SEM.values()) + DMA_SEMS}

        p = Prog()
        t.par = 0
        t.weave = []
        t.dpar = 0
        emit_setup(p, t)
        for c in range(CH):
            p.op("sync", lambda E, c=c: E.dma_start(
                t.x_sb[:, c * D:(c + 1) * D], t.x_d[c * 128:(c + 1) * 128, :]),
                 writes=["x:%d" % c], sem="qx%d" % c)
        emit_weight_split(p, t)
        for k in range(nt):
            emit_tile(p, t, k)
        for c in range(4):
            p.stream["gpsimd"].append(("wait", "qo%d" % c, p.cnt["qo%d" % c]))
        p.materialize(nc, sems)
    return nc



def emit_setup(p, t):
    dts = t.hg[:, 0:128]
    iop1 = t.hg[:, 128:256]
    maskut = t.hg[:, 256:384]
    loads = [(t.ident[:, :], t.c_ident, "ident"), (dts, t.c_dts, "hg:0"),
             (iop1, t.c_iotap1, "hg:1"), (maskut, t.c_maskut, "hg:2"),
             (t.oh127[:, :], t.c_oh127, "oh127"), (t.ones1[:, :], t.c_ones, "ones1"),
             (t.lamv[:, :], t.c_decay_beta, "lamv")]
    for s, d, key in loads:
        p.op("sync", (lambda E, s=s, d=d: E.dma_start(s, d[:, :])), writes=[key],
             sem="qc")
    p.op("sync", lambda E: E.dma_start(t.scr[0:1, :], t.c_lngamma[:, :]), writes=["scr"],
         sem="qc")

    p.op("vector", lambda E: E.memset(t.small[:, 3:4], LN_EPS), writes=["epsv"])
    p.op("scalar", lambda E: E.activation(t.lamv[:, :], t.lamv[:, :], AF.Exp, scale=-LN2),
         reads=["lamv"], writes=["lamv"])
    p.op("vector", lambda E: E.tensor_scalar(t.lamv[:, :], t.lamv[:, :], 1.0, -1.0,
                                             op0=ALU.subtract, op1=ALU.mult),
         reads=["lamv"], writes=["lamv"])
    p.op("vector", lambda E: E.tensor_scalar(t.lamv[:, :], t.lamv[:, :], TINY, 1.0 - 1e-9,
                                             op0=ALU.max, op1=ALU.min),
         reads=["lamv"], writes=["lamv"])
    p.op("scalar", lambda E: E.activation(t.lnl[:, :], t.lamv[:, :], AF.Ln),
         reads=["lamv"], writes=["lnl"])

    p.op("tensor", lambda E: E.matmul(t.pb[0][:, 0:NH], t.ones1[0:1, :], t.lnl[0:1, :],
                                      start=True, stop=True),
         reads=["lnl", "ones1"], writes=["pb0"])
    p.op("vector", lambda E: E.tensor_copy(t.lnl_bc[:, :], t.pb[0][:, 0:NH]),
         reads=["pb0"], writes=["lnl_bc"])
    for j in range(4):
        s0, s1 = j * 512, (j + 1) * 512
        p.op("tensor", lambda E, s0=s0, s1=s1: E.matmul(
            t.pb[1][:, :], t.ones1[0:1, :], t.scr[0:1, s0:s1], start=True, stop=True),
             reads=["scr", "ones1"], writes=["pb1"])
        p.op("vector", lambda E, s0=s0, s1=s1: E.tensor_copy(t.gbc[:, s0:s1], t.pb[1][:, :]),
             reads=["pb1"], writes=["gbc"])
    p.op("sync", lambda E: E.dma_start(t.scr[0:1, :], t.c_lnbeta[:, :]), writes=["scr"],
         sem="qc")
    for j in range(4):
        s0, s1 = j * 512, (j + 1) * 512
        p.op("tensor", lambda E, s0=s0, s1=s1: E.matmul(
            t.pb[2][:, :], t.ones1[0:1, :], t.scr[0:1, s0:s1], start=True, stop=True),
             reads=["scr", "ones1"], writes=["pb2"])
        p.op("vector", lambda E, s0=s0, s1=s1: E.tensor_copy(t.bbc[:, s0:s1], t.pb[2][:, :]),
             reads=["pb2"], writes=["bbc"])

    a3 = t.a_t[:, :].rearrange("p (h u) -> p h u", h=NH)
    s3 = t.sp[:, :].rearrange("p (h u) -> p h u", h=NH)
    lnl3 = _bc(t.lnl_bc[:, :], 2, 128)
    p.op("vector", lambda E, dts=dts: E.tensor_tensor(a3, _bc(dts, 1, NH), lnl3, op=ALU.mult),
         reads=["hg:0", "lnl_bc"], writes=["a_t"])
    p.op("scalar", lambda E: E.activation(t.a_t[:, :], t.a_t[:, :], AF.Exp),
         reads=["a_t"], writes=["a_t"])
    p.op("vector", lambda E, maskut=maskut: E.tensor_tensor(a3, a3, _bc(maskut, 1, NH), op=ALU.mult),
         reads=["a_t", "hg:2"], writes=["a_t"])
    p.op("vector", lambda E, iop1=iop1: E.tensor_tensor(s3, _bc(iop1, 1, NH), lnl3, op=ALU.mult),
         reads=["hg:1", "lnl_bc"], writes=["sp"])
    p.op("scalar", lambda E: E.activation(t.sp[:, :], t.sp[:, :], AF.Exp),
         reads=["sp"], writes=["sp"])
    oh3 = bass.AP(t.oh127, 0, [list(t.oh127[:, :].ap[0]), [0, NH], [0, 128]])
    p.op("vector", lambda E: E.tensor_tensor(s3, s3, oh3, op=ALU.mult),
         reads=["sp", "oh127"], writes=["sp"])


def emit_weight_split(p, t):
    ji = 0
    for src_d, dst in ((t.wqt_d, t.wqt_s), (t.wvt_d, t.wvt_s),
                       (t.wgt_d, t.wgt_s), (t.wot_d, t.wot_s),
                       (t.wga_d, t.wga_s), (t.wup_d, t.wup_s),
                       (t.wdn_d, t.wdn_s)):
        rows, cols = src_d.shape
        for kr in range(rows // 128):
            for kc in range(cols // 2048):
                sl = ji % 4
                ji += 1
                stg = t.va[:, sl * 2048:(sl + 1) * 2048]
                host = t.lnH if sl < 2 else t.lnL
                base = (sl % 2) * 4096
                pair = host[:, base:base + 4096]
                hiv = bass.AP(host, pair.offset, [list(pair.ap[0]), [1024, 4], [1, 512]])
                lov = bass.AP(host, pair.offset + 512, [list(pair.ap[0]), [1024, 4], [1, 512]])
                sgv = bass.AP(t.va, stg.offset, [list(stg.ap[0]), [512, 4], [1, 512]])
                kin = ["va:%d" % sl]
                kh = "lnH" if sl < 2 else "lnL"
                kpair = ["%s:%d" % (kh, base + j * 128) for j in range(32)]
                sap = src_d[kr * 128:(kr + 1) * 128, kc * 2048:(kc + 1) * 2048]
                p.op("sync", lambda E, stg=stg, sap=sap: E.dma_start(stg, sap),
                     writes=kin, sem="qsi%d" % sl)
                p.op("scalar", lambda E, hiv=hiv, sgv=sgv: E.copy(hiv, sgv),
                     reads=kin, writes=kpair[:16])
                p.op("vector", lambda E, lov=lov, sgv=sgv, hiv=hiv: E.tensor_tensor(
                    lov, sgv, hiv, op=ALU.subtract),
                     reads=kin + kpair[:1], writes=kpair[16:])
                off = kr * 128 * 2 * cols + kc * 4096
                dap = bass.AP(dst, off, [[2 * cols, 128], [1, 4096]])
                p.op("gpsimd", lambda E, dap=dap, pair=pair: E.dma_start(dap, pair),
                     reads=kpair, writes=["sW:%s" % dst.name], sem="qsp%d" % sl)



def _layernorm(p, t, src, src_keys, dst, dst_keys, affine=False):
    for j in range(4):
        sj = bass.AP(src.tensor, src.offset + j * 512, [list(src.ap[0]), [1, 512]])
        p.op("vector", lambda E, sj=sj, j=j: E.bn_stats(t.st6[:, j * 6:(j + 1) * 6], sj),
             reads=src_keys, writes=["st6:%d" % j])
    p.op("vector", lambda E: E.bn_aggr(t.small[:, 0:2], t.st6[:, :]),
         reads=["st6:%d" % j for j in range(4)], writes=["ag2"])
    p.op("scalar", lambda E: E.activation(t.small[:, 2:3], t.small[:, 1:2], AF.Sqrt,
                                          bias=t.small[:, 3:4]),
         reads=["ag2", "epsv"], writes=["rstd"])
    p.op("vector", lambda E: E.reciprocal(t.small[:, 2:3], t.small[:, 2:3]),
         reads=["rstd"], writes=["rstd"])
    p.op("vector", lambda E: E.tensor_scalar(dst, src, t.small[:, 0:1], t.small[:, 2:3],
                                             op0=ALU.subtract, op1=ALU.mult),
         reads=list(src_keys) + ["ag2", "rstd"], writes=list(dst_keys))
    if affine:
        p.op("vector", lambda E: E.tensor_tensor(dst, dst, t.gbc[:, :], op=ALU.mult),
             reads=list(dst_keys) + ["gbc"], writes=list(dst_keys))
        p.op("vector", lambda E: E.tensor_tensor(dst, dst, t.bbc[:, :], op=ALU.add),
             reads=list(dst_keys) + ["bbc"], writes=list(dst_keys))


def _transpose_grp(p, t, src, src_keys, c, grp):
    bank = 4 + (grp % 2)
    fns = []
    for j in range(4):
        dc = grp * 4 + j
        sblk = bass.AP(src.tensor, src.offset + dc * 128,
                       [list(src.ap[0]), [1, 128]])
        fns.append(lambda E, sblk=sblk, j=j, bank=bank: E.transpose(
            t.pb[bank][:, j * 128:(j + 1) * 128], sblk, t.ident[:, :]))
    p.op("tensor", fns, reads=list(src_keys) + ["ident"], writes=["pb%d" % bank])
    cols = [((grp * 4 + j) * CH + c) * 128 for j in range(4)]
    hk = ["lnH:%d" % co for co in cols]
    lk = ["lnL:%d" % co for co in cols]

    def evh(E, bank=bank, cols=cols):
        inst = None
        for j, co in enumerate(cols):
            inst = E.copy(t.lnH[:, co:co + 128],
                          t.pb[bank][:, j * 128:(j + 1) * 128])
        return inst

    def evl(E, bank=bank, cols=cols):
        inst = None
        for j, co in enumerate(cols):
            inst = E.tensor_tensor(t.lnL[:, co:co + 128],
                                   t.pb[bank][:, j * 128:(j + 1) * 128],
                                   t.lnH[:, co:co + 128], op=ALU.subtract)
        return inst
    p.op("scalar", evh, reads=["pb%d" % bank], writes=hk)
    p.op("vector", evl, reads=["pb%d" % bank] + hk, writes=lk)


def _transpose_band(p, t, src, src_keys, c):
    for grp in range(4):
        _transpose_grp(p, t, src, src_keys, c, grp)


def _proj_closures(p, t, wsd, colbase, itile, evac, bank_base=0):
    cls = []
    for dq in range(8):
        def go(dq=dq):
            pq = t.par % 3
            t.par += 1
            grp = (colbase + itile * 512) // 512
            dap = bass.AP(wsd, dq * 256 * 4096 + grp * 1024,
                          [[4096, 128], [128 * 4096, 2], [1, 1024]])
            sap = t.wa[:, pq * 2048:(pq + 1) * 2048].rearrange(
                "p (c j) -> p c j", c=2)
            p.op("sync", lambda E, dap=dap, sap=sap: E.dma_start(sap, dap),
                 reads=["sW:%s" % wsd.name],
                 writes=["wa:%d" % pq], sem="qwa%d" % pq)
            fns = []
            rk = ["wa:%d" % pq]
            for dl in range(2):
                dc = dq * 2 + dl
                for tt in range(CH):
                    co = (dc * CH + tt) * 128
                    lhsH = t.lnH[:, co:co + 128]
                    rhsH = t.wa[:, pq * 2048 + dl * 1024: pq * 2048 + dl * 1024 + 512]
                    rhsL = t.wa[:, pq * 2048 + dl * 1024 + 512: pq * 2048 + dl * 1024 + 1024]
                    st = (dc == 0)
                    fns.append(lambda E, lhsH=lhsH, rhsH=rhsH, tt=tt, st=st: E.matmul(
                        t.pb[bank_base + tt][:, :], lhsH, rhsH, start=st, stop=False))
                    fns.append(lambda E, lhsH=lhsH, rhsL=rhsL, tt=tt: E.matmul(
                        t.pb[bank_base + tt][:, :], lhsH, rhsL, start=False, stop=False))
                    rk += ["lnH:%d" % co, "lnL:%d" % co]
            for dl in range(2):
                dc = dq * 2 + dl
                for tt in range(CH):
                    co = (dc * CH + tt) * 128
                    lhsL = t.lnL[:, co:co + 128]
                    rhsH = t.wa[:, pq * 2048 + dl * 1024: pq * 2048 + dl * 1024 + 512]
                    sp_ = (dc == 15)
                    fns.append(lambda E, lhsL=lhsL, rhsH=rhsH, tt=tt, sp_=sp_: E.matmul(
                        t.pb[bank_base + tt][:, :], lhsL, rhsH, start=False, stop=sp_))
            p.op("tensor", fns, reads=rk,
                 writes=["pb%d" % (bank_base + tt) for tt in range(CH)])
        cls.append(go)

    def evacs():
        for tt in range(CH):
            evac(tt, itile, bank_base)
        for _ in range(12):
            if t.weave:
                t.weave.pop(0)()
    cls.append(evacs)
    return cls


def _proj_unit(p, t, wsd, colbase, itile, evac, bank_base=0):
    for c in _proj_closures(p, t, wsd, colbase, itile, evac, bank_base):
        c()



def emit_tile(p, t, k):
    tok0 = k * TILE
    seq_tile = k % TPS

    def xk(c):
        return "x:%d" % c

    def vak(c):
        return "va:%d" % c

    xband = lambda c: t.x_sb[:, c * D:(c + 1) * D]
    vband = lambda c: t.va[:, c * D:(c + 1) * D]

    if seq_tile != 0:
        p.op("gpsimd", lambda E: E.dma_start(t.scr[:, :], t.retc_d[:, :]),
             writes=["scr"], sem="qrc")
    if k != 0:
        for c in range(CH):
            p.op("sync", lambda E, c=c: E.dma_start(
                xband(c), t.x_d[tok0 + c * 128: tok0 + (c + 1) * 128, :]), writes=[xk(c)],
                 sem="qx%d" % c)
    for c in range(CH):
        _layernorm(p, t, xband(c), [xk(c)], vband(c), [vak(c)])
    def evac_v(tt, itile, bb=0):
        sl = t.va[:, tt * D + itile * 512: tt * D + (itile + 1) * 512]
        p.op("scalar", lambda E, sl=sl, b=bb + tt: E.copy(sl, t.pb[b][:, :]),
             reads=["pb%d" % (bb + tt)], writes=[vak(tt)])

    v0 = _proj_closures(p, t, t.wvt_s, 0, 0, evac_v)
    for c in range(CH):
        _transpose_grp(p, t, vband(c), [vak(c)], c, 0)
    for dq in range(8):
        v0[dq]()
        if dq % 2 == 1 and dq < 7:
            for c in range(CH):
                _transpose_grp(p, t, vband(c), [vak(c)], c, (dq + 1) // 2)
    v0[8]()
    for itile in range(1, 4):
        _proj_unit(p, t, t.wvt_s, 0, itile, evac_v, bank_base=4 * (itile % 2))

    def ret_chunk(c):
        seq_start = (seq_tile == 0 and c == 0)
        fns = []
        reads = [vak(c), "a_t", "sp"]
        for b in range(4):
            for hh in range(4):
                h = 4 * b + hh
                vsl = t.va[:, c * D + h * 128: c * D + (h + 1) * 128]
                psl = t.pb[4 + b][:, hh * 128:(hh + 1) * 128]
                asl = t.a_t[:, h * 128:(h + 1) * 128]
                fns.append(lambda E, psl=psl, asl=asl, vsl=vsl, hh=hh, ss=seq_start:
                           E.matmul(psl, asl, vsl, start=(hh == 0),
                                    stop=(hh == 3 and ss)))
                if not seq_start:
                    prev = (t.va[:, (c - 1) * D + h * 128: (c - 1) * D + (h + 1) * 128]
                            if c > 0 else t.scr[:, h * 128:(h + 1) * 128])
                    ssl = t.sp[:, h * 128:(h + 1) * 128]
                    fns.append(lambda E, psl=psl, ssl=ssl, prev=prev, hh=hh:
                               E.matmul(psl, ssl, prev, start=False, stop=(hh == 3)))
        if not seq_start:
            reads.append(vak(c - 1) if c > 0 else "scr")
        p.op("tensor", fns, reads=reads, writes=["pb%d" % (4 + b) for b in range(4)])
        for b in range(4):
            sl = t.va[:, c * D + b * 512: c * D + (b + 1) * 512]
            p.op("scalar", lambda E, sl=sl, b=b: E.copy(sl, t.pb[4 + b][:, :]),
                 reads=["pb%d" % (4 + b)], writes=[vak(c)])
    def ret_carry():
        if seq_tile != TPS - 1:
            p.op("gpsimd", lambda E: E.dma_start(t.retc_d[:, :], vband(3)),
                 reads=[vak(3)], writes=["retcd"], sem="qrc")

    def evac_q(tt, itile, bb=0):
        sl = t.va[:, tt * D + itile * 512: tt * D + (itile + 1) * 512]
        p.op("vector", lambda E, sl=sl, b=bb + tt: E.tensor_tensor(
            sl, t.pb[b][:, :], sl, op=ALU.mult),
             reads=["pb%d" % (bb + tt), vak(tt)], writes=[vak(tt)])

    q0 = _proj_closures(p, t, t.wqt_s, 0, 0, evac_q)
    ret_chunk(0)
    q0[0]()
    q0[1]()
    ret_chunk(1)
    q0[2]()
    q0[3]()
    ret_chunk(2)
    q0[4]()
    q0[5]()
    ret_chunk(3)
    ret_carry()
    q0[6]()
    q0[7]()
    q0[8]()
    for itile in range(1, 4):
        _proj_unit(p, t, t.wqt_s, 0, itile, evac_q, bank_base=4 * (itile % 2))

    for c in range(CH):
        _layernorm(p, t, vband(c), [vak(c)], vband(c), [vak(c)], affine=True)

    def evac_g(tt, itile, bb=0):
        gsl = t.scr[:, tt * 512:(tt + 1) * 512]
        osl = t.va[:, tt * D + itile * 512: tt * D + (itile + 1) * 512]
        p.op("scalar", lambda E, gsl=gsl, b=bb + tt: E.activation(gsl, t.pb[b][:, :], AF.Silu),
             reads=["pb%d" % (bb + tt)], writes=["scr"]),
        p.op("vector", lambda E, osl=osl, gsl=gsl: E.tensor_tensor(
            osl, osl, gsl, op=ALU.mult),
             reads=[vak(tt), "scr"], writes=[vak(tt)])

    for itile in range(4):
        _proj_unit(p, t, t.wgt_s, 0, itile, evac_g, bank_base=4 * (itile % 2))

    def evac_o(tt, itile, bb=0):
        xsl = t.x_sb[:, tt * D + itile * 512: tt * D + (itile + 1) * 512]
        p.op("vector", lambda E, xsl=xsl, b=bb + tt: E.tensor_tensor(
            xsl, t.pb[b][:, :], xsl, op=ALU.add),
             reads=["pb%d" % (bb + tt), xk(tt)], writes=[xk(tt)])

    o0 = _proj_closures(p, t, t.wot_s, 0, 0, evac_o)
    for c in range(CH):
        _transpose_grp(p, t, vband(c), [vak(c)], c, 0)
    for dq in range(8):
        o0[dq]()
        if dq % 2 == 1 and dq < 7:
            for c in range(CH):
                _transpose_grp(p, t, vband(c), [vak(c)], c, (dq + 1) // 2)
    o0[8]()
    for itile in range(1, 4):
        _proj_unit(p, t, t.wot_s, 0, itile, evac_o, bank_base=4 * (itile % 2))

    for c in range(CH):
        _layernorm(p, t, xband(c), [xk(c)], vband(c), [vak(c)])
    for c in range(CH):
        _transpose_band(p, t, vband(c), [vak(c)], c)

    for tp in range(2):
        ts = (2 * tp, 2 * tp + 1)

        def gu(s, half, ts=ts):
            ig, ist = divmod(s, 4)
            colbase = ig * 2048 + ist * 512
            for dq in range(4 * half, 4 * half + 4):
                t.par += 1
                for wsd, wbuf, wk, qs_, bb, npar in (
                        (t.wga_s, t.wa, "wa", "qwa", 0, 3),
                        (t.wup_s, t.wb, "wb", "qwb", 2, 2)):
                    pq = t.par % npar
                    dap = bass.AP(wsd, dq * 256 * 16384 + (colbase // 512) * 1024,
                                  [[16384, 128], [128 * 16384, 2], [1, 1024]])
                    sap = wbuf[:, pq * 2048:(pq + 1) * 2048].rearrange(
                        "p (c j) -> p c j", c=2)
                    p.op("sync", lambda E, dap=dap, sap=sap: E.dma_start(sap, dap),
                         reads=["sW:%s" % wsd.name],
                         writes=["%s:%d" % (wk, pq)], sem="%s%d" % (qs_, pq))
                    fns = []
                    rk = ["%s:%d" % (wk, pq)]
                    for dl in range(2):
                        dc = dq * 2 + dl
                        for tl in range(2):
                            tt_ = ts[tl]
                            co = (dc * CH + tt_) * 128
                            lhsH = t.lnH[:, co:co + 128]
                            rhsH = wbuf[:, pq * 2048 + dl * 1024: pq * 2048 + dl * 1024 + 512]
                            rhsL = wbuf[:, pq * 2048 + dl * 1024 + 512: pq * 2048 + dl * 1024 + 1024]
                            st = (dc == 0)
                            fns.append(lambda E, lhsH=lhsH, rhsH=rhsH, tl=tl, st=st, bb=bb:
                                       E.matmul(t.pb[bb + tl][:, :], lhsH, rhsH,
                                                start=st, stop=False))
                            fns.append(lambda E, lhsH=lhsH, rhsL=rhsL, tl=tl, bb=bb:
                                       E.matmul(t.pb[bb + tl][:, :], lhsH, rhsL,
                                                start=False, stop=False))
                            rk += ["lnH:%d" % co, "lnL:%d" % co]
                    for dl in range(2):
                        dc = dq * 2 + dl
                        for tl in range(2):
                            tt_ = ts[tl]
                            co = (dc * CH + tt_) * 128
                            lhsL = t.lnL[:, co:co + 128]
                            rhsH = wbuf[:, pq * 2048 + dl * 1024: pq * 2048 + dl * 1024 + 512]
                            sp_ = (dc == 15)
                            fns.append(lambda E, lhsL=lhsL, rhsH=rhsH, tl=tl, sp_=sp_, bb=bb:
                                       E.matmul(t.pb[bb + tl][:, :], lhsL, rhsH,
                                                start=False, stop=sp_))
                    p.op("tensor", fns, reads=rk,
                         writes=["pb%d" % (bb + tl) for tl in range(2)])

        def eps(s):
            ig, ist = divmod(s, 4)
            for tl in range(2):
                hsl = t.hg[:, tl * D + ist * 512: tl * D + (ist + 1) * 512]
                hkeys = ["hg:%d" % (tl * DC + ist * 4 + j) for j in range(4)]
                p.op("scalar", lambda E, hsl=hsl, tl=tl: E.activation(
                    hsl, t.pb[tl][:, :], AF.Silu),
                     reads=["pb%d" % tl], writes=hkeys)
                p.op("vector", lambda E, hsl=hsl, tl=tl: E.tensor_tensor(
                    hsl, hsl, t.pb[2 + tl][:, :], op=ALU.mult),
                     reads=["pb%d" % (2 + tl)] + hkeys, writes=hkeys)

        def spars(s):
            ig, ist = divmod(s, 4)
            for tl in range(2):
                _sparsify_sub(p, t, tl, ist)

        def trans(s):
            ig, ist = divmod(s, 4)
            for tl in range(2):
                bank = 4 + tl
                fns = []
                for j in range(4):
                    ic = ist * 4 + j
                    hsl = t.hg[:, tl * D + ic * 128: tl * D + (ic + 1) * 128]
                    fns.append(lambda E, hsl=hsl, j=j, bank=bank: E.transpose(
                        t.pb[bank][:, j * 128:(j + 1) * 128], hsl, t.ident[:, :]))
                bkeys = ["hg:%d" % (tl * DC + ist * 4 + j) for j in range(4)]
                p.op("tensor", fns, reads=bkeys + ["ident"], writes=["pb%d" % bank])
                hco = tl * D + ist * 512
                hk = ["hsH:%d" % (tl * DC + ist * 4 + j) for j in range(4)]
                lk = ["hsL:%d" % (tl * DC + ist * 4 + j) for j in range(4)]
                p.op("scalar", lambda E, bank=bank, hco=hco: E.copy(
                    t.hsH[:, hco:hco + 512], t.pb[bank][:, :]),
                     reads=["pb%d" % bank], writes=hk)
                p.op("vector", lambda E, bank=bank, hco=hco: E.tensor_tensor(
                    t.hsL[:, hco:hco + 512], t.pb[bank][:, :],
                    t.hsH[:, hco:hco + 512], op=ALU.subtract),
                     reads=["pb%d" % bank] + hk, writes=lk)

        def down(s, nh, ts=ts):
            ig, ist = divmod(s, 4)
            if True:
                for icp in range(2):
                    pq = t.dpar % 2
                    t.dpar += 1
                    r0 = ig * 2048 + (ist * 4 + icp * 2) * 128
                    dap = bass.AP(t.wdn_s, r0 * 4096 + nh * 2048,
                                  [[4096, 128], [128 * 4096, 2], [1, 2048]])
                    sap = t.wd[:, pq * 4096:(pq + 1) * 4096].rearrange(
                        "p (c j) -> p c j", c=2)
                    p.op("gpsimd", lambda E, dap=dap, sap=sap: E.dma_start(sap, dap),
                         reads=["sW:%s" % t.wdn_s.name],
                         writes=["wd:%d" % pq], sem="qwd%d" % pq)
                    fns = []
                    rks = ["wd:%d" % pq]
                    for icl2 in range(2):
                        icl = icp * 2 + icl2
                        ic = ist * 4 + icl
                        rks += ["hsH:%d" % ic, "hsL:%d" % ic,
                                "hsH:%d" % (DC + ic), "hsL:%d" % (DC + ic)]
                        for tl in range(2):
                            hco = tl * D + ic * 128
                            lhsH = t.hsH[:, hco:hco + 128]
                            lhsL = t.hsL[:, hco:hco + 128]
                            for nl in range(2):
                                bank = 4 + tl * 2 + nl
                                rhsH = t.wd[:, pq * 4096 + icl2 * 2048 + nl * 1024:
                                            pq * 4096 + icl2 * 2048 + nl * 1024 + 512]
                                rhsL = t.wd[:, pq * 4096 + icl2 * 2048 + nl * 1024 + 512:
                                            pq * 4096 + icl2 * 2048 + nl * 1024 + 1024]
                                fns.append(lambda E, lhsH=lhsH, rhsH=rhsH, bank=bank, icl=icl:
                                           E.matmul(t.pb[bank][:, :], lhsH, rhsH,
                                                    start=(icl == 0), stop=False))
                                fns.append(lambda E, lhsH=lhsH, rhsL=rhsL, bank=bank:
                                           E.matmul(t.pb[bank][:, :], lhsH, rhsL,
                                                    start=False, stop=False))
                            for nl in range(2):
                                bank = 4 + tl * 2 + nl
                                rhsH = t.wd[:, pq * 4096 + icl2 * 2048 + nl * 1024:
                                            pq * 4096 + icl2 * 2048 + nl * 1024 + 512]
                                fns.append(lambda E, lhsL=lhsL, rhsH=rhsH, bank=bank, icl=icl:
                                           E.matmul(t.pb[bank][:, :], lhsL, rhsH,
                                                    start=False, stop=(icl == 3)))
                    p.op("tensor", fns, reads=rks,
                         writes=["pb%d" % (4 + j) for j in range(4)])
                for tl in range(2):
                    tt_ = ts[tl]
                    for nl in range(2):
                        n = nh * 2 + nl
                        bank = 4 + tl * 2 + nl
                        ysl = t.va[:, tt_ * D + n * 512: tt_ * D + (n + 1) * 512]
                        if s == 0:
                            p.op("scalar", lambda E, ysl=ysl, bank=bank: E.copy(
                                ysl, t.pb[bank][:, :]),
                                 reads=["pb%d" % bank], writes=[vak(tt_)])
                        else:
                            p.op("vector", lambda E, ysl=ysl, bank=bank: E.tensor_tensor(
                                ysl, ysl, t.pb[bank][:, :], op=ALU.add),
                                 reads=["pb%d" % bank, vak(tt_)], writes=[vak(tt_)])

        for s in range(16):
            if s == 0:
                gu(0, 0)
                gu(0, 1)
                eps(0)
                spars(0)
            if s + 1 < 16:
                gu(s + 1, 0)
            trans(s)
            if s + 1 < 16:
                gu(s + 1, 1)
            down(s, 0)
            down(s, 1)
            if s + 1 < 16:
                eps(s + 1)
                spars(s + 1)

    for c in range(CH):
        p.op("vector", lambda E, c=c: E.tensor_tensor(
            vband(c), vband(c), xband(c), op=ALU.add),
             reads=[vak(c), xk(c)], writes=[vak(c)])
        p.op("gpsimd", lambda E, c=c: E.dma_start(
            t.out_d[tok0 + c * 128: tok0 + (c + 1) * 128, :], vband(c)),
             reads=[vak(c)], writes=["out:%d:%d" % (k, c)], sem="qo%d" % c)


def _sparsify_sub(p, t, tl, ist):
    col0 = tl * D + ist * 512
    sub = t.hg[:, col0:col0 + 512]
    bkeys = ["hg:%d" % (tl * DC + ist * 4 + j) for j in range(4)]
    p.op("scalar", lambda E: E.activation(t.scr[:, 0:512], sub, AF.Abs),
         reads=bkeys, writes=["scr"])
    a = [_stride4(t.scr, 0, i, 128) for i in range(4)]
    w0 = w1 = t.wt[:, 0:128]
    acc = [t.scg[:, i * 128:(i + 1) * 128] for i in range(4)]
    ak = ["scg:%d" % i for i in range(4)]

    def tt(dst, x, y, op, dk, rk):
        p.op("vector", lambda E, dst=dst, x=x, y=y, op=op: E.tensor_tensor(
            dst, x, y, op=op), reads=rk, writes=dk)

    tt(w0, a[0], a[1], ALU.is_ge, ["wt0"], ["scr"])
    p.op("vector", lambda E: E.tensor_copy(acc[0], w0), reads=["wt0"], writes=[ak[0]])
    p.op("vector", lambda E: E.tensor_scalar(acc[1], w0, -1.0, None, op0=ALU.mult),
         reads=["wt0"], writes=[ak[1]])
    tt(w1, a[0], a[2], ALU.is_ge, ["wt0"], ["scr"])
    tt(acc[0], acc[0], w1, ALU.add, [ak[0]], [ak[0], "wt0"])
    p.op("vector", lambda E: E.tensor_scalar(acc[2], w1, -1.0, None, op0=ALU.mult),
         reads=["wt0"], writes=[ak[2]])
    tt(w0, a[0], a[3], ALU.is_ge, ["wt0"], ["scr"])
    tt(acc[0], acc[0], w0, ALU.add, [ak[0]], [ak[0], "wt0"])
    p.op("vector", lambda E: E.tensor_copy(acc[3], w0), reads=["wt0"], writes=[ak[3]])
    tt(w1, a[1], a[2], ALU.is_ge, ["wt0"], ["scr"])
    tt(acc[1], acc[1], w1, ALU.add, [ak[1]], [ak[1], "wt0"])
    tt(acc[2], acc[2], w1, ALU.subtract, [ak[2]], [ak[2], "wt0"])
    tt(w0, a[1], a[3], ALU.is_ge, ["wt0"], ["scr"])
    tt(acc[1], acc[1], w0, ALU.add, [ak[1]], [ak[1], "wt0"])
    tt(acc[3], acc[3], w0, ALU.add, [ak[3]], [ak[3], "wt0"])
    tt(w1, a[2], a[3], ALU.is_ge, ["wt0"], ["scr"])
    tt(acc[2], acc[2], w1, ALU.add, [ak[2]], [ak[2], "wt0"])
    tt(acc[3], acc[3], w1, ALU.add, [ak[3]], [ak[3], "wt0"])
    specs = [(0, ALU.is_ge, 2.0), (1, ALU.is_ge, 1.0), (2, ALU.is_ge, 0.0),
             (3, ALU.is_le, 1.0)]
    for i, cmp, thr in specs:
        hi = _stride4(t.hg, col0, i, 128)
        p.op("vector", lambda E, hi=hi, ai=acc[i], cmp=cmp, thr=thr:
             E.scalar_tensor_tensor(hi, ai, thr, hi, op0=cmp, op1=ALU.mult),
             reads=[ak[i]] + bkeys, writes=bkeys)



def _host_inputs(x_shard, ins):
    m = {
        "x": np.ascontiguousarray(np.asarray(x_shard, np.float32).reshape(TOK, D)),
        "wqt": np.ascontiguousarray(np.asarray(ins["L0_attn_Wq"], np.float32).T),
        "wvt": np.ascontiguousarray(np.asarray(ins["L0_attn_Wv"], np.float32).T),
        "wgt": np.ascontiguousarray(np.asarray(ins["L0_attn_Wg"], np.float32).T),
        "wot": np.ascontiguousarray(np.asarray(ins["L0_attn_Wo"], np.float32).T),
        "wgate": np.ascontiguousarray(np.asarray(ins["L0_ffn_w_gate"], np.float32)),
        "wup": np.ascontiguousarray(np.asarray(ins["L0_ffn_w_up"], np.float32)),
        "wdown": np.ascontiguousarray(np.asarray(ins["L0_ffn_w_down"], np.float32)),
        "c_ident": np.eye(128, dtype=np.float32),
        "c_dts": (np.arange(128, dtype=np.float32)[None, :]
                  - np.arange(128, dtype=np.float32)[:, None]),
        "c_iotap1": np.broadcast_to(np.arange(1, 129, dtype=np.float32)[None, :],
                                    (128, 128)).copy(),
        "c_maskut": np.triu(np.ones((128, 128), dtype=np.float32)),
        "c_oh127": (np.arange(128) == 127).astype(np.float32)[:, None].copy(),
        "c_ones": np.ones((1, 128), dtype=np.float32),
        "c_decay_beta": np.asarray(ins["L0_attn_beta"], np.float32).reshape(1, NH),
        "c_lngamma": np.asarray(ins["L0_attn_ln_gamma"], np.float32).reshape(1, D),
        "c_lnbeta": np.asarray(ins["L0_attn_ln_beta"], np.float32).reshape(1, D),
    }
    return m


def kernel(**inputs):
    x = np.asarray(inputs["L0_attn_x"], np.float32)
    nc = build_nc()
    per = B // NCORES
    shared = _host_inputs(x[0:per], inputs)
    in_maps = []
    for i in range(NCORES):
        m = dict(shared)
        m["x"] = np.ascontiguousarray(x[i * per:(i + 1) * per].reshape(TOK, D))
        in_maps.append(m)
    res = run_bass_kernel_spmd(nc, in_maps, core_ids=list(range(NCORES)))
    shards = [np.asarray(r["out"], np.float32).reshape(per, L, D) for r in res.results]
    return np.concatenate(shards, axis=0)
